# version 09
# baseline (speedup 1.0000x reference)
import contextlib

import numpy as np
import ml_dtypes

import concourse.bass as bass
import concourse.mybir as mybir
from concourse.bass_utils import run_bass_kernel_spmd

F32 = mybir.dt.float32
BF16 = mybir.dt.bfloat16
OP = mybir.AluOpType
ACT = mybir.ActivationFunctionType
AX = mybir.AxisListType
BF = ml_dtypes.bfloat16
AP = bass.AP

B, S, D = 4, 4096, 2048
NCH = 16
TC = 2048
NTAU = 2
TT = 1024
TH = 512
NG = 4
NBLK = 48
E, F = 8, 1024
NFB = 8
LN_EPS = 1e-5
RMS_EPS = 1e-6

PERM = np.concatenate([np.arange(0, 1024, 2), np.arange(1, 1024, 2),
                       np.arange(1024, 2048)]).astype(np.int64)


def _win_col_order():
    cols, kinds = [], []
    for q in range(4):
        cols.append(2048 + PERM[q * 128:(q + 1) * 128]); kinds.append(("KE", q))
        cols.append(2048 + PERM[512 + q * 128:512 + (q + 1) * 128]); kinds.append(("KO", q))
        cols.append(4096 + PERM[q * 128:(q + 1) * 128]); kinds.append(("VE", q))
        cols.append(4096 + PERM[512 + q * 128:512 + (q + 1) * 128]); kinds.append(("VO", q))
    for j in range(8):
        cols.append(2048 + PERM[1024 + j * 128:1024 + (j + 1) * 128]); kinds.append(("KP", j))
        cols.append(4096 + PERM[1024 + j * 128:1024 + (j + 1) * 128]); kinds.append(("VP", j))
    for i in range(16):
        cols.append(PERM[i * 128:(i + 1) * 128]); kinds.append(("Q", i))
    return np.concatenate(cols), kinds


WIN_COLS, WIN_KINDS = _win_col_order()


def _blocked_lhs(w, n_out_blocks):
    K, M = w.shape
    return np.ascontiguousarray(
        w.reshape(K // 128, 128, n_out_blocks, 128).transpose(2, 1, 0, 3))


def make_host_inputs(inp):
    f32 = lambda k: np.ascontiguousarray(np.asarray(inp[k]), dtype=np.float32)
    X = f32("L0_attn_X")
    W_in = f32("L0_attn_W_in"); b_in = f32("L0_attn_b_in")
    W_out = f32("L0_attn_W_out"); b_out = f32("L0_attn_b_out")
    gamma = f32("L0_attn_gamma"); beta = f32("L0_attn_beta")
    rmsw = f32("L0_ffn_rms_weight"); router_w = f32("L0_ffn_router_w")
    w_gate = f32("L0_ffn_w_gate"); w_up = f32("L0_ffn_w_up")
    w_down = f32("L0_ffn_w_down")
    sw_gate = f32("L0_ffn_sw_gate"); sw_up = f32("L0_ffn_sw_up")
    sw_down = f32("L0_ffn_sw_down")
    c_mod = f32("L0_c"); wmod = f32("L0_wmod")

    sh = {}
    sh["w_in_blk"] = _blocked_lhs(W_in[:, WIN_COLS], NBLK).astype(BF)
    sh["w_out_blk"] = _blocked_lhs(W_out[PERM, :], 16).astype(BF)
    sh["wg_blk"] = np.stack([_blocked_lhs(w_gate[e], NFB) for e in range(E)]).astype(BF)
    sh["wu_blk"] = np.stack([_blocked_lhs(w_up[e], NFB) for e in range(E)]).astype(BF)
    sh["wd_blk"] = np.stack([_blocked_lhs(w_down[e], 16) for e in range(E)]).astype(BF)
    sh["swg_blk"] = _blocked_lhs(sw_gate, NFB).astype(BF)
    sh["swu_blk"] = _blocked_lhs(sw_up, NFB).astype(BF)
    sh["swd_blk"] = _blocked_lhs(sw_down, 16).astype(BF)
    sh["router_blk"] = np.ascontiguousarray(
        router_w.reshape(NCH, 128, 8).transpose(1, 0, 2)).astype(BF)
    sh["b_qkv"] = np.ascontiguousarray(b_in[WIN_COLS].reshape(NBLK, 128).T)
    for name, v in (("gamma_t", gamma), ("beta_t", beta),
                    ("rmsw_t", rmsw), ("b_out_t", b_out)):
        sh[name] = np.ascontiguousarray(v.reshape(NCH, 128).T)
    sh["ones_bf"] = np.ones((128, 128), dtype=BF)
    sel16 = np.zeros((128, 8 * 128), dtype=BF)
    for e in range(8):
        sel16[np.arange(e, 128, 8), e * 128:(e + 1) * 128] = BF(1.0 / 16.0)
    sh["sel16_bf"] = sel16
    sh["ones1_bf"] = np.ones((1, 128), dtype=BF)
    sh["ident"] = np.eye(128, dtype=np.float32)
    sh["ident_bf"] = np.eye(128, dtype=BF)

    inv_freq = 1.0 / (10000.0 ** (np.arange(0, 1024, 2, dtype=np.float64) / 1024.0))
    ang = np.arange(S, dtype=np.float64)[None, :] * inv_freq[:, None]
    cos_t = np.cos(ang).astype(np.float32)
    sin_t = np.sin(ang).astype(np.float32)

    in_maps = []
    for core in range(8):
        b = core // 2
        sl = slice((core % 2) * TC, (core % 2) * TC + TC)
        m = dict(sh)
        m["x_fm"] = np.ascontiguousarray(X[b].T.reshape(NCH, 128, S)[:, :, sl])
        m["cos_sh"] = np.ascontiguousarray(cos_t[:, sl].reshape(4, 128, TC))
        m["sin_sh"] = np.ascontiguousarray(sin_t[:, sl].reshape(4, 128, TC))
        m["wmod_blk"] = np.ascontiguousarray(
            wmod[core * 256:(core + 1) * 256, :]
            .reshape(2, 128, 96, 128).transpose(1, 0, 2, 3)).astype(BF)
        m["c_slice"] = np.ascontiguousarray(
            c_mod[:, core * 256:(core + 1) * 256].T.reshape(2, 128, 4))
        selv = np.zeros((128, 4), dtype=np.float32)
        selv[:, b] = 1.0
        m["sel4"] = selv
        in_maps.append(m)
    return in_maps


def assemble_output(results):
    out = np.empty((B, S, D), dtype=np.float32)
    for core in range(8):
        b, shh = core // 2, core % 2
        o = np.asarray(results[core]["out_fm"])
        out[b, shh * TC:(shh + 1) * TC, :] = o.reshape(D, TC).T
    return out


class Sem:
    def __init__(self, es, nc, name):
        self.h = es.enter_context(nc.semaphore(name))
        self.v = 0

    def inc(self, instr, n=1):
        instr.then_inc(self.h, n)
        self.v += n
        return self.v


def build_program(taps=(), phases=("P0", "A", "A2", "AF", "B", "C1", "C2")):
    phases = set(phases)
    nc = bass.Bass("TRN2", target_bir_lowering=False,
                   detect_race_conditions=False)
    dt = nc.dram_tensor

    x_fm = dt("x_fm", [NCH, 128, TC], F32, kind="ExternalInput")
    w_in_d = dt("w_in_blk", [NBLK, 128, NCH, 128], BF16, kind="ExternalInput")
    w_out_d = dt("w_out_blk", [16, 128, NCH, 128], BF16, kind="ExternalInput")
    wg_d = dt("wg_blk", [E, NFB, 128, NCH, 128], BF16, kind="ExternalInput")
    wu_d = dt("wu_blk", [E, NFB, 128, NCH, 128], BF16, kind="ExternalInput")
    wd_d = dt("wd_blk", [E, 16, 128, NFB, 128], BF16, kind="ExternalInput")
    swg_d = dt("swg_blk", [NFB, 128, NCH, 128], BF16, kind="ExternalInput")
    swu_d = dt("swu_blk", [NFB, 128, NCH, 128], BF16, kind="ExternalInput")
    swd_d = dt("swd_blk", [16, 128, NFB, 128], BF16, kind="ExternalInput")
    router_d = dt("router_blk", [128, NCH, 8], BF16, kind="ExternalInput")
    bqkv_d = dt("b_qkv", [128, NBLK], F32, kind="ExternalInput")
    gamma_d = dt("gamma_t", [128, NCH], F32, kind="ExternalInput")
    beta_d = dt("beta_t", [128, NCH], F32, kind="ExternalInput")
    rmsw_d = dt("rmsw_t", [128, NCH], F32, kind="ExternalInput")
    bout_d = dt("b_out_t", [128, NCH], F32, kind="ExternalInput")
    ones_d = dt("ones_bf", [128, 128], BF16, kind="ExternalInput")
    ones1_d = dt("ones1_bf", [1, 128], BF16, kind="ExternalInput")
    sel16_d = dt("sel16_bf", [128, 8 * 128], BF16, kind="ExternalInput")
    ident_d = dt("ident", [128, 128], F32, kind="ExternalInput")
    identbf_d = dt("ident_bf", [128, 128], BF16, kind="ExternalInput")
    cos_d = dt("cos_sh", [4, 128, TC], F32, kind="ExternalInput")
    sin_d = dt("sin_sh", [4, 128, TC], F32, kind="ExternalInput")
    wmod_d = dt("wmod_blk", [128, 2, 96, 128], BF16, kind="ExternalInput")
    c_d = dt("c_slice", [2, 128, 4], F32, kind="ExternalInput")
    sel_d = dt("sel4", [128, 4], F32, kind="ExternalInput")

    out_fm = dt("out_fm", [NCH, 128, TC], F32, kind="ExternalOutput")

    q_spill = dt("q_spill", [NTAU, 16, 2, 128, TH], BF16)
    xh_spill = dt("xh_spill", [NG, NCH, 128, TH], BF16)
    cc_mod_in = dt("cc_mod_in", [128, 384], F32)
    cc_mod_out = dt("cc_mod_out", [128, 384], F32)
    cc_a_in = dt("cc_a_in", [128, NCH], F32)
    cc_a_out = dt("cc_a_out", [128, NCH], F32)

    tap_outs = {}

    def tap(name, shape, dtype=F32):
        if name in taps and name not in tap_outs:
            tap_outs[name] = dt("tap_" + name, shape, dtype, kind="ExternalOutput")
        return tap_outs.get(name)

    es = contextlib.ExitStack()
    with es:
        def sbuf(name, p, n, dtype):
            return es.enter_context(nc.sbuf_tensor(name, [p, n], dtype))

        ones_t = sbuf("ones_t", 128, 128, BF16)
        ones1_t = sbuf("ones1_t", 1, 128, BF16)
        sel16_t = sbuf("sel16_t", 128, 8 * 128, BF16)
        ident_t = sbuf("ident_t", 128, 128, F32)
        identbf_t = sbuf("identbf_t", 128, 128, BF16)
        bqkv_t = sbuf("bqkv_t", 128, NBLK, F32)
        gamma_t = sbuf("gamma_sb", 128, NCH, F32)
        beta_t = sbuf("beta_sb", 128, NCH, F32)
        rmsw_t = sbuf("rmsw_sb", 128, NCH, F32)
        bout_t = sbuf("bout_sb", 128, NCH, F32)
        modt = sbuf("modt", 128, 96, F32)
        ca1_t = sbuf("ca1_t", 128, NCH, F32)
        cb1_t = sbuf("cb1_t", 128, NCH, F32)
        gab_t = sbuf("gab_t", 128, NCH, F32)
        acol_t = sbuf("acol_t", 128, NCH * NTAU, F32)
        afin_t = sbuf("afin_t", 128, NCH, F32)
        epsln_t = sbuf("epsln_t", 128, 1, F32)
        epsrms_t = sbuf("epsrms_t", 128, 1, F32)
        zcol_t = sbuf("zcol_t", 128, 1, F32)

        sW = Sem(es, nc, "sW")
        sX = Sem(es, nc, "sX")
        sSp = Sem(es, nc, "sSp")
        sPE = Sem(es, nc, "sPE")
        sV = Sem(es, nc, "sV")
        sSc = Sem(es, nc, "sSc")
        sCC = Sem(es, nc, "sCC")
        sEv = Sem(es, nc, "sEv")
        sO = Sem(es, nc, "sO")

        ps = [es.enter_context(nc.psum_tensor(f"ps{i}", [128, TH], F32))
              for i in range(8)]

        M_SA, M_CA, M_GA, M_SB, M_CB, M_GB = 0, 16, 32, 48, 64, 80
        sa_c = lambda kc: modt[:, M_SA + kc:M_SA + kc + 1]
        ga_c = lambda kc: modt[:, M_GA + kc:M_GA + kc + 1]
        sb_c = lambda kc: modt[:, M_SB + kc:M_SB + kc + 1]
        gb_c = lambda kc: modt[:, M_GB + kc:M_GB + kc + 1]

        with nc.sbuf_tensor("wmod_t", [128, 2 * 96 * 128], BF16) as wmod_t, \
             nc.sbuf_tensor("c_t", [128, 8], F32) as c_t, \
             nc.sbuf_tensor("cs_t", [128, 8], BF16) as cs_t, \
             nc.sbuf_tensor("sel_t", [128, 4], F32) as sel_t, \
             nc.sbuf_tensor("modev", [128, 384], F32) as modev_t, \
             nc.sbuf_tensor("modrd", [128, 384], F32) as modrd_t, \
             nc.Block() as blk:

            @blk.sync
            def _(sync):
                for dst, src in ((ones_t, ones_d), (ones1_t, ones1_d),
                                 (sel16_t, sel16_d),
                                 (ident_t, ident_d), (identbf_t, identbf_d),
                                 (bqkv_t, bqkv_d),
                                 (gamma_t, gamma_d), (beta_t, beta_d),
                                 (rmsw_t, rmsw_d), (bout_t, bout_d),
                                 (sel_t, sel_d)):
                    sync.dma_start(out=dst[:, :], in_=src.ap()).then_inc(sX.h, 16)
                    sX.v += 16
                sync.dma_start(out=c_t[:, :],
                               in_=AP(c_d, 0, [[4, 128], [512, 2], [1, 4]])
                               ).then_inc(sX.h, 16)
                sX.v += 16
            c_ready = sX.v

            @blk.gpsimd
            def _(gpsimd):
                gpsimd.memset(epsln_t[:, :], LN_EPS)
                gpsimd.memset(epsrms_t[:, :], RMS_EPS)
                gpsimd.memset(zcol_t[:, :], 0.0)
                gpsimd.dma_start(
                    out=wmod_t[:, :],
                    in_=AP(wmod_d, 0, [[2 * 96 * 128, 128], [1, 2 * 96 * 128]]),
                ).then_inc(sW.h, 16)
                sW.v += 16
            w_ready0 = sW.v

            @blk.scalar
            def _(scalar):
                scalar.wait_ge(sX.h, c_ready)
                sSc.inc(scalar.activation(cs_t[:, :], c_t[:, :], ACT.Silu))
            silu_done = sSc.v

            @blk.tensor
            def _(tensor):
                tensor.wait_ge(sSc.h, silu_done)
                tensor.wait_ge(sW.h, w_ready0)
                mm = None
                for jb in range(96):
                    for kc in range(2):
                        mm = tensor.matmul(
                            ps[0][:, jb * 4:(jb + 1) * 4],
                            wmod_t[:, (kc * 96 + jb) * 128:(kc * 96 + jb + 1) * 128],
                            cs_t[:, kc * 4:(kc + 1) * 4],
                            start=(kc == 0), stop=(kc == 1))
                sPE.inc(mm)
            mod_pe = sPE.v

            @blk.vector
            def _(vector):
                vector.wait_ge(sPE.h, mod_pe)
                sV.inc(vector.tensor_copy(modev_t[:, :], ps[0][:, 0:384]))
            mod_ev = sV.v

            @blk.gpsimd
            def _(gpsimd):
                gpsimd.wait_ge(sV.h, mod_ev)
                gpsimd.dma_start(out=cc_mod_in.ap(), in_=modev_t[:, :]).then_inc(sW.h, 16)
                sW.v += 16
                gpsimd.wait_ge(sW.h, sW.v)
                sCC.inc(gpsimd.collective_compute(
                    "AllReduce", OP.add, replica_groups=[list(range(8))],
                    ins=[cc_mod_in.ap()], outs=[cc_mod_out.ap()]))
                gpsimd.wait_ge(sCC.h, sCC.v)
                gpsimd.dma_start(out=modrd_t[:, :], in_=cc_mod_out.ap()).then_inc(sW.h, 16)
                sW.v += 16
            mod_back = sW.v

            @blk.vector
            def _(vector):
                vector.wait_ge(sW.h, mod_back)
                vector.tensor_tensor(
                    modrd_t[:, :], modrd_t[:, :],
                    AP(sel_t, 0, [[4, 128], [0, 96], [1, 4]]), OP.mult)
                vector.reduce_sum(
                    modt[:, :], AP(modrd_t, 0, [[384, 128], [4, 96], [1, 4]]),
                    axis=AX.X)
                gA = modt[:, M_GA:M_GA + 16]
                gB = modt[:, M_GB:M_GB + 16]
                t1A = modrd_t[:, 0:16]; t2A = modrd_t[:, 16:32]
                t3A = modrd_t[:, 32:48]
                t1B = modrd_t[:, 48:64]; t2B = modrd_t[:, 64:80]
                t3B = modrd_t[:, 80:96]
                vector.engine_nop()
                vector.engine_nop()
                vector.tensor_single_scalar(t3A, gA, 0.0, OP.is_ge)
                vector.tensor_single_scalar(t3B, gB, 0.0, OP.is_ge)
                vector.tensor_scalar_max(t1A, gA, 0.25)
                vector.tensor_scalar_max(t1B, gB, 0.25)
                vector.tensor_scalar_min(t2A, gA, -0.25)
                vector.tensor_scalar_min(t2B, gB, -0.25)
                vector.tensor_scalar(t3A, t3A, 0.5, -0.25, OP.mult, OP.add)
                vector.tensor_scalar(t3B, t3B, 0.5, -0.25, OP.mult, OP.add)
                vector.tensor_add(t1A, t1A, t2A)
                vector.tensor_add(t1B, t1B, t2B)
                vector.tensor_scalar_add(ca1_t[:, :], modt[:, M_CA:M_CA + 16],
                                         1.0)
                vector.tensor_add(gA, t1A, t3A)
                vector.tensor_scalar_add(cb1_t[:, :], modt[:, M_CB:M_CB + 16],
                                         1.0)
                vector.tensor_add(gB, t1B, t3B)
                vector.engine_nop()
                vector.engine_nop()
                sV.inc(vector.tensor_mul(gab_t[:, :], modt[:, M_GA:M_GA + 16],
                                         bout_t[:, :]))
            mod_done = sV.v

            if tap("modt", [128, 96]):
                @blk.sync
                def _(sync):
                    sync.wait_ge(sV.h, mod_done)
                    sync.dma_start(out=tap_outs["modt"].ap(),
                                   in_=modt[:, :]).then_inc(sO.h, 16)
                    sO.v += 16

        with nc.sbuf_tensor("hn_t", [128, NCH * TT], BF16) as hn_t, \
             nc.sbuf_tensor("p_t", [128, NCH * TT], BF16) as p_t:

            for tau in (range(NTAU) if "A" in phases else []):
                with nc.sbuf_tensor(f"x_t{tau}", [128, 2 * NCH * TH], F32) as x_t, \
                     nc.sbuf_tensor(f"h_t{tau}", [128, NCH * TH], BF16) as h_t, \
                     nc.sbuf_tensor(f"xh_t{tau}", [128, NCH * TH], BF16) as xh_t, \
                     nc.sbuf_tensor(f"sq_t{tau}", [128, NCH * TH], BF16) as sq_t, \
                     nc.sbuf_tensor(f"st_t{tau}", [128, 6 * TH], F32) as st_t, \
                     nc.Block() as blk:

                    x_ready = []

                    @blk.sync
                    def _(sync):
                        for st in range(2):
                            g = tau * 2 + st
                            sync.dma_start(
                                out=x_t[:, st * NCH * TH:(st + 1) * NCH * TH],
                                in_=AP(x_fm, g * TH,
                                       [[TC, 128], [128 * TC, NCH], [1, TH]]),
                            ).then_inc(sX.h, 16)
                            sX.v += 16
                            x_ready.append(sX.v)

                    for st in range(2):
                        g = tau * 2 + st
                        xs = lambda kc, st=st: x_t[:, (st * NCH + kc) * TH:(st * NCH + kc + 1) * TH]
                        s_mu = st_t[:, 0:TH]
                        s_var = st_t[:, TH:2 * TH]
                        s_rst = st_t[:, 2 * TH:3 * TH]
                        s_mur = st_t[:, 3 * TH:4 * TH]
                        s_t2 = st_t[:, 4 * TH:5 * TH]
                        s_tx = st_t[:, 5 * TH:6 * TH]

                        @blk.scalar
                        def _(scalar):
                            scalar.wait_ge(sX.h, x_ready[st])
                            if st == 1:
                                scalar.wait_ge(sV.h, sV.v)
                            src = x_t[:, st * NCH * TH:(st + 1) * NCH * TH]
                            sSc.inc(scalar.copy(h_t[:, :], src))
                            sSc.inc(scalar.activation(sq_t[:, :], src, ACT.Square))
                        sc_x = sSc.v

                        @blk.tensor
                        def _(tensor):
                            tensor.wait_ge(sSc.h, sc_x)
                            mm = None
                            for kc in range(NCH):
                                mm = tensor.matmul(ps[0][:, :], ones_t[:, :],
                                                   h_t[:, kc * TH:(kc + 1) * TH],
                                                   start=(kc == 0), stop=(kc == NCH - 1))
                            for kc in range(NCH):
                                mm = tensor.matmul(ps[1][:, :], ones_t[:, :],
                                                   sq_t[:, kc * TH:(kc + 1) * TH],
                                                   start=(kc == 0), stop=(kc == NCH - 1))
                            sPE.inc(mm)
                        pe_st = sPE.v

                        @blk.vector
                        def _(vector):
                            vector.wait_ge(sPE.h, pe_st)
                            vector.tensor_scalar_mul(s_mu, ps[0][:, :], 1.0 / D)
                            vector.tensor_scalar_mul(s_var, ps[1][:, :], 1.0 / D)
                            vector.tensor_mul(s_t2, s_mu, s_mu)
                            sV.inc(vector.tensor_sub(s_var, s_var, s_t2))
                        v_var = sV.v

                        @blk.scalar
                        def _(scalar):
                            scalar.wait_ge(sV.h, v_var)
                            sSc.inc(scalar.activation(s_rst, s_var, ACT.Sqrt,
                                                      bias=epsln_t[:, :]))
                        sc_std = sSc.v

                        @blk.vector
                        def _(vector):
                            vector.wait_ge(sSc.h, sc_std)
                            vector.reciprocal(s_rst, s_rst)
                            vector.tensor_mul(s_mur, s_mu, s_rst)
                            for kc in range(NCH):
                                vector.tensor_tensor(s_t2, xs(kc), s_rst, OP.mult)
                                vector.tensor_sub(s_t2, s_t2, s_mur)
                                vector.tensor_scalar(
                                    h_t[:, kc * TH:(kc + 1) * TH], s_t2,
                                    ca1_t[:, kc:kc + 1], sa_c(kc), OP.mult, OP.add)
                                vector.scalar_tensor_tensor(
                                    xh_t[:, kc * TH:(kc + 1) * TH],
                                    h_t[:, kc * TH:(kc + 1) * TH],
                                    ga_c(kc), xs(kc), OP.mult, OP.add)
                            sV.inc(vector.engine_nop())
                        v_h = sV.v

                        @blk.sync
                        def _(sync):
                            sync.wait_ge(sV.h, v_h)
                            if g >= 2:
                                pass
                            sync.dma_start(
                                out=AP(xh_spill, g * NCH * 128 * TH,
                                       [[TH, 128], [128 * TH, NCH], [1, TH]]),
                                in_=xh_t[:, :]).then_inc(sSp.h, 16)
                            sSp.v += 16

                        @blk.scalar
                        def _(scalar):
                            scalar.wait_ge(sV.h, v_h)
                            sSc.inc(scalar.activation(sq_t[:, :], h_t[:, :], ACT.Square))
                        sc_hsq = sSc.v

                        @blk.tensor
                        def _(tensor):
                            tensor.wait_ge(sSc.h, sc_hsq)
                            mm = None
                            for kc in range(NCH):
                                mm = tensor.matmul(ps[2][:, :], ones_t[:, :],
                                                   h_t[:, kc * TH:(kc + 1) * TH],
                                                   start=(kc == 0), stop=(kc == NCH - 1))
                            for kc in range(NCH):
                                mm = tensor.matmul(ps[3][:, :], ones_t[:, :],
                                                   sq_t[:, kc * TH:(kc + 1) * TH],
                                                   start=(kc == 0), stop=(kc == NCH - 1))
                            sPE.inc(mm)
                        pe_hst = sPE.v

                        @blk.vector
                        def _(vector):
                            vector.wait_ge(sPE.h, pe_hst)
                            vector.tensor_scalar_mul(s_mu, ps[2][:, :], 1.0 / D)
                            vector.tensor_scalar_mul(s_var, ps[3][:, :], 1.0 / D)
                            vector.tensor_mul(s_t2, s_mu, s_mu)
                            sV.inc(vector.tensor_sub(s_var, s_var, s_t2))
                        v_hvar = sV.v

                        @blk.scalar
                        def _(scalar):
                            scalar.wait_ge(sV.h, v_hvar)
                            sSc.inc(scalar.activation(s_rst, s_var, ACT.Sqrt,
                                                      bias=epsln_t[:, :]))
                        sc_hstd = sSc.v

                        @blk.vector
                        def _(vector):
                            vector.wait_ge(sSc.h, sc_hstd)
                            vector.wait_ge(sSp.h, sSp.v)
                            vector.reciprocal(s_rst, s_rst)
                            vector.tensor_mul(s_mur, s_mu, s_rst)
                            for kc in range(NCH):
                                hc = h_t[:, kc * TH:(kc + 1) * TH]
                                vector.tensor_tensor(s_t2, hc, s_rst, OP.mult)
                                vector.tensor_sub(s_t2, s_t2, s_mur)
                                vector.tensor_scalar(
                                    hn_t[:, kc * TT + st * TH:kc * TT + (st + 1) * TH],
                                    s_t2, gamma_t[:, kc:kc + 1], beta_t[:, kc:kc + 1],
                                    OP.mult, OP.add)
                            sV.inc(vector.engine_nop())

                        if tau == 0 and st == 0 and tap("h", [128, NCH * TH], BF16):
                            @blk.sync
                            def _(sync):
                                sync.wait_ge(sV.h, sV.v)
                                sync.dma_start(out=tap_outs["h"].ap(),
                                               in_=h_t[:, :]).then_inc(sO.h, 16)
                                sO.v += 16

                if tau == 0 and tap("hn", [128, NCH * TT], BF16):
                    with nc.Block() as blk:
                        @blk.sync
                        def _(sync):
                            sync.dma_start(out=tap_outs["hn"].ap(),
                                           in_=hn_t[:, :]).then_inc(sO.h, 16)
                            sO.v += 16

                RING = 4
                with nc.sbuf_tensor(f"w_t{tau}", [128, RING * NCH * 128], BF16) as w_t, \
                     nc.sbuf_tensor(f"quad_t{tau}", [128, 4 * TT], F32) as quad_t, \
                     nc.sbuf_tensor(f"rsc_t{tau}", [128, 2 * TT], F32) as rsc_t, \
                     nc.sbuf_tensor(f"tab_t{tau}", [128, 4 * TT], F32) as tab_t, \
                     nc.sbuf_tensor(f"sqs_t{tau}", [128, 2 * TT], BF16) as sqs_t, \
                     nc.sbuf_tensor(f"qev_t{tau}", [128, 2 * TT], BF16) as qev_t, \
                     nc.sbuf_tensor(f"wvec_t{tau}", [128, TT], F32) as wvec_t, \
                     nc.sbuf_tensor(f"junk_t{tau}", [128, TT], F32) as junk_t, \
                     nc.Block() as blk:

                    wslice = lambda r, kc: w_t[:, (r * NCH + kc) * 128:(r * NCH + kc + 1) * 128]
                    quad = lambda s: quad_t[:, s * TT:(s + 1) * TT]
                    quad_h = lambda s, th: quad_t[:, s * TT + th * TH:s * TT + (th + 1) * TH]
                    sqs = lambda r: sqs_t[:, r * TT:(r + 1) * TT]
                    sqs_h = lambda r, th: sqs_t[:, r * TT + th * TH:r * TT + (th + 1) * TH]
                    qev = lambda r: qev_t[:, r * TT:(r + 1) * TT]
                    qev_h = lambda r, th: qev_t[:, r * TT + th * TH:r * TT + (th + 1) * TH]
                    tabc = lambda q: tab_t[:, (q % 2) * 2 * TT:(q % 2) * 2 * TT + TT]
                    tabs = lambda q: tab_t[:, ((q % 2) * 2 + 1) * TT:((q % 2) * 2 + 2) * TT]

                    w_ready = {}
                    pe_blk = {}
                    ev_blk = {}
                    sqmm_done = {}
                    sq_done = {}
                    tab_ready = {}
                    rot_done = {}
                    qspill = {}
                    k2cnt = [0]
                    v2cnt = [0]

                    def emit_wdma(b):
                        @blk.gpsimd
                        def _(gpsimd):
                            if b >= RING:
                                gpsimd.wait_ge(sPE.h, pe_blk[b - RING])
                            gpsimd.dma_start(
                                out=w_t[:, (b % RING) * NCH * 128:(b % RING + 1) * NCH * 128],
                                in_=AP(w_in_d, b * 128 * NCH * 128,
                                       [[NCH * 128, 128], [1, NCH * 128]]),
                            ).then_inc(sW.h, 16)
                            sW.v += 16
                            w_ready[b] = sW.v

                    if "A2" in phases:
                        for b in range(RING):
                            emit_wdma(b)

                    for b in (range(NBLK) if "A2" in phases else []):
                        kind, idx = WIN_KINDS[b]

                        if kind == "KE":
                            q = idx

                            @blk.sync
                            def _(sync):
                                if q >= 2:
                                    sync.wait_ge(sV.h, rot_done[q - 2])
                                sync.dma_start(
                                    out=tabc(q),
                                    in_=AP(cos_d, q * 128 * TC + tau * TT,
                                           [[TC, 128], [1, TT]])).then_inc(sX.h, 16)
                                sX.v += 16
                                sync.dma_start(
                                    out=tabs(q),
                                    in_=AP(sin_d, q * 128 * TC + tau * TT,
                                           [[TC, 128], [1, TT]])).then_inc(sX.h, 16)
                                sX.v += 16
                                tab_ready[q] = sX.v

                        @blk.tensor
                        def _(tensor):
                            tensor.wait_ge(sW.h, w_ready[b])
                            if b >= 2:
                                tensor.wait_ge(sEv.h, ev_blk[b - 2])
                            mm = None
                            for kc in range(NCH):
                                lhs = wslice(b % RING, kc)
                                for th in range(2):
                                    mm = tensor.matmul(
                                        ps[(2 * b + th) % 4][:, :], lhs,
                                        hn_t[:, kc * TT + th * TH:kc * TT + (th + 1) * TH],
                                        start=(kc == 0), stop=(kc == NCH - 1))
                            pe_blk[b] = sPE.inc(mm)

                        if b + RING < NBLK:
                            emit_wdma(b + RING)

                        if kind == "Q" and "noq" not in phases:
                            i = idx

                            @blk.scalar
                            def _(scalar):
                                scalar.wait_ge(sPE.h, pe_blk[b])
                                if i >= 2:
                                    scalar.wait_ge(sSp.h, qspill[i - 2])
                                a = None
                                for th in range(2):
                                    a = scalar.activation(
                                        qev_h(i % 2, th), ps[(2 * b + th) % 4][:, :],
                                        ACT.Identity, bias=bqkv_t[:, b:b + 1], scale=1.0)
                                ev_blk[b] = sEv.inc(a)

                            @blk.sync
                            def _(sync):
                                sync.wait_ge(sEv.h, ev_blk[b])
                                for th in range(2):
                                    sync.dma_start(
                                        out=AP(q_spill,
                                               ((tau * 16 + i) * 2 + th) * 128 * TH,
                                               [[TH, 128], [1, TH]]),
                                        in_=qev_h(i % 2, th)).then_inc(sSp.h, 16)
                                    sSp.v += 16
                                qspill[i] = sSp.v
                        elif kind == "Q":
                            @blk.vector
                            def _(vector):
                                vector.wait_ge(sPE.h, pe_blk[b])
                                a = None
                                for th in range(2):
                                    a = vector.tensor_copy(
                                        junk_t[:, th * TH:(th + 1) * TH],
                                        ps[(2 * b + th) % 4][:, :])
                                ev_blk[b] = sEv.inc(a)
                        else:
                            slot = {"KE": 0, "KO": 1, "VE": 2, "VO": 3,
                                    "KP": 0, "VP": 1}[kind]

                            @blk.vector
                            def _(vector):
                                vector.wait_ge(sPE.h, pe_blk[b])
                                if kind in ("KP", "VP") and idx >= 1:
                                    vector.wait_ge(sSc.h, sq_done.get(("P", idx - 1), 0))
                                elif kind in ("KP", "VP"):
                                    vector.wait_ge(
                                        sSc.h,
                                        sq_done.get(12 if kind == "KP" else 13, 0))
                                elif idx >= 1:
                                    vector.wait_ge(sSc.h, sq_done.get(b - 4, 0))
                                a = None
                                for th in range(2):
                                    a = vector.tensor_scalar(
                                        quad_h(slot, th), ps[(2 * b + th) % 4][:, :],
                                        bqkv_t[:, b:b + 1], None, OP.add)
                                ev_blk[b] = sEv.inc(a)

                            if "nosq" not in phases:
                                @blk.scalar
                                def _(scalar):
                                    scalar.wait_ge(sEv.h, ev_blk[b])
                                    if b >= 2:
                                        scalar.wait_ge(sPE.h, sqmm_done.get(b - 2, 0))
                                    sSc.inc(scalar.activation(sqs(b % 2), quad(slot),
                                                              ACT.Square))
                                sq_done[b] = sSc.v
                                if kind in ("KP", "VP"):
                                    sq_done[("P", idx)] = sSc.v

                                @blk.tensor
                                def _(tensor):
                                    tensor.wait_ge(sSc.h, sq_done[b])
                                    cnt = k2cnt if kind.startswith("K") else v2cnt
                                    base = 4 if kind.startswith("K") else 6
                                    mm = None
                                    for th in range(2):
                                        mm = tensor.matmul(
                                            ps[base + th][:, :], ones_t[:, :],
                                            sqs_h(b % 2, th),
                                            start=(cnt[0] == 0), stop=(cnt[0] == 15))
                                    cnt[0] += 1
                                    sqmm_done[b] = sPE.inc(mm)
                            else:
                                sq_done[b] = sSc.v
                                if kind in ("KP", "VP"):
                                    sq_done[("P", idx)] = sSc.v

                        if kind == "VO" and "norot" not in phases:
                            q = idx

                            @blk.vector
                            def _(vector):
                                vector.wait_ge(sSc.h, sq_done[b])
                                vector.wait_ge(sX.h, tab_ready[q])
                                c = tabc(q); s = tabs(q)
                                r0 = rsc_t[:, 0:TT]; r1 = rsc_t[:, TT:2 * TT]
                                vector.tensor_tensor(r0, quad(0), s, OP.mult)
                                vector.tensor_tensor(r1, quad(0), c, OP.mult)
                                vector.tensor_tensor(quad(0), quad(1), s, OP.mult)
                                vector.tensor_sub(quad(0), r1, quad(0))
                                vector.tensor_tensor(quad(1), quad(1), c, OP.mult)
                                vector.tensor_add(quad(1), quad(1), r0)
                                vector.tensor_tensor(r0, quad(2), s, OP.mult)
                                vector.tensor_tensor(r1, quad(2), c, OP.mult)
                                vector.tensor_tensor(quad(2), quad(3), s, OP.mult)
                                vector.tensor_sub(quad(2), r1, quad(2))
                                vector.tensor_tensor(quad(3), quad(3), c, OP.mult)
                                vector.tensor_add(quad(3), quad(3), r0)
                                vector.tensor_tensor(
                                    p_t[:, q * TT:(q + 1) * TT], quad(0), quad(2), OP.mult)
                                a = vector.tensor_tensor(
                                    p_t[:, (4 + q) * TT:(5 + q) * TT], quad(1), quad(3),
                                    OP.mult)
                                rot_done[q] = sV.inc(a)

                        if kind == "VO" and "norot" in phases:
                            rot_done[idx] = sV.v
                        if kind == "VP" and "norot" not in phases:
                            j = idx

                            @blk.vector
                            def _(vector):
                                vector.wait_ge(sSc.h, sq_done[b])
                                sV.inc(vector.tensor_tensor(
                                    p_t[:, (8 + j) * TT:(9 + j) * TT],
                                    quad(0), quad(1), OP.mult))

                        if b == 31 and "noacc" not in phases and "nosq" not in phases:
                            @blk.vector
                            def _(vector):
                                vector.wait_ge(sPE.h, sqmm_done[31])
                                a = None
                                for th in range(2):
                                    a = vector.tensor_copy(
                                        rsc_t[:, th * TH:(th + 1) * TH],
                                        ps[4 + th][:, :])
                                    a = vector.tensor_copy(
                                        rsc_t[:, TT + th * TH:TT + (th + 1) * TH],
                                        ps[6 + th][:, :])
                                sV.inc(a)
                            v_k2 = sV.v

                            @blk.scalar
                            def _(scalar):
                                scalar.wait_ge(sV.h, v_k2)
                                scalar.activation(junk_t[:, :], rsc_t[:, 0:TT],
                                                  ACT.Sqrt)
                                sSc.inc(scalar.activation(
                                    wvec_t[:, :], rsc_t[:, TT:2 * TT], ACT.Sqrt))
                            sc_nrm = sSc.v

                            @blk.vector
                            def _(vector):
                                vector.wait_ge(sSc.h, sc_nrm)
                                vector.tensor_mul(wvec_t[:, :], wvec_t[:, :],
                                                  junk_t[:, :])
                                vector.reciprocal(wvec_t[:, :], wvec_t[:, :])
                                a = None
                                for kc in range(NCH):
                                    vector.tensor_tensor(
                                        junk_t[:, :], p_t[:, kc * TT:(kc + 1) * TT],
                                        wvec_t[:, :], OP.mult)
                                    a = vector.reduce_sum(
                                        acol_t[:, tau * NCH + kc:tau * NCH + kc + 1],
                                        junk_t[:, :], axis=AX.X)
                                sV.inc(a)

                    if tau == 0 and tap("kprime", [128, TT]):
                        @blk.sync
                        def _(sync):
                            sync.wait_ge(sV.h, sV.v)
                            sync.dma_start(out=tap_outs["kprime"].ap(),
                                           in_=wvec_t[:, :]).then_inc(sO.h, 16)
                            sO.v += 16

                    if tau == 0 and tap("pt", [128, NCH * TT], BF16):
                        @blk.sync
                        def _(sync):
                            sync.wait_ge(sV.h, sV.v)
                            sync.dma_start(out=tap_outs["pt"].ap(),
                                           in_=p_t[:, :]).then_inc(sO.h, 16)
                            sO.v += 16

                    if tau == 1 and tap("acol", [128, NCH * NTAU]):
                        @blk.sync
                        def _(sync):
                            sync.wait_ge(sV.h, sV.v)
                            sync.dma_start(out=tap_outs["acol"].ap(),
                                           in_=acol_t[:, :]).then_inc(sO.h, 16)
                            sO.v += 16

        with nc.Block() as blk:
            @blk.vector
            def _(vector):
                if "AF" not in phases:
                    return
                sV.inc(vector.tensor_add(afin_t[:, :], acol_t[:, 0:NCH],
                                         acol_t[:, NCH:2 * NCH]))
            a_part = sV.v

            @blk.gpsimd
            def _(gpsimd):
                if "AF" not in phases:
                    return
                gpsimd.wait_ge(sV.h, a_part)
                gpsimd.dma_start(out=cc_a_in.ap(), in_=afin_t[:, :]).then_inc(sW.h, 16)
                sW.v += 16
                gpsimd.wait_ge(sW.h, sW.v)
                sCC.inc(gpsimd.collective_compute(
                    "AllReduce", OP.add,
                    replica_groups=[[0, 1], [2, 3], [4, 5], [6, 7]],
                    ins=[cc_a_in.ap()], outs=[cc_a_out.ap()]))
                gpsimd.wait_ge(sCC.h, sCC.v)
                gpsimd.dma_start(out=afin_t[:, :], in_=cc_a_out.ap()).then_inc(sW.h, 16)
                sW.v += 16
            a_back = sW.v

            @blk.vector
            def _(vector):
                if "AF" not in phases:
                    return
                vector.wait_ge(sW.h, a_back)
                sV.inc(vector.tensor_scalar(afin_t[:, :], afin_t[:, :], 0.5, 0.5,
                                            OP.mult, OP.add))

            if tap("afin", [128, NCH]):
                @blk.sync
                def _(sync):
                    sync.wait_ge(sV.h, sV.v)
                    sync.dma_start(out=tap_outs["afin"].ap(),
                                   in_=afin_t[:, :]).then_inc(sO.h, 16)
                    sO.v += 16

        with nc.sbuf_tensor("x2_t", [128, NCH * TC], BF16) as x2_t:

            with nc.sbuf_tensor("aq_t", [128, NCH * TC], BF16) as aq_t, \
                 nc.sbuf_tensor("woutb", [128, 2 * NCH * 128], BF16) as woutb_t, \
                 nc.sbuf_tensor("xhb_t", [128, 2 * 4 * TH], BF16) as xhb_t, \
                 nc.sbuf_tensor("junkB", [128, TH], F32) as junkB_t, \
                 nc.Block() as blk:

                q_ready = []

                @blk.sync
                def _(sync):
                    if "B" not in phases:
                        q_ready.append(sX.v)
                        return
                    for tau in range(NTAU):
                        for th in range(2):
                            sync.dma_start(
                                out=AP(aq_t, tau * TT + th * TH,
                                       [[NCH * TC, 128], [TC, NCH], [1, TH]]),
                                in_=AP(q_spill, (tau * 32 + th) * 128 * TH,
                                       [[TH, 128], [2 * 128 * TH, 16], [1, TH]]),
                            ).then_inc(sX.h, 16)
                            sX.v += 16
                    q_ready.append(sX.v)

                @blk.vector
                def _(vector):
                    if "B" not in phases:
                        return
                    vector.wait_ge(sX.h, q_ready[0])
                    a = None
                    for kc in range(NCH):
                        a = vector.tensor_scalar_mul(
                            aq_t[:, kc * TC:(kc + 1) * TC],
                            aq_t[:, kc * TC:(kc + 1) * TC],
                            afin_t[:, kc:kc + 1])
                    sV.inc(a)
                aq_done = sV.v

                wout_ready = {}
                peB = {}
                evB = {}
                xh_ready = {}

                def emit_wout(ob):
                    @blk.gpsimd
                    def _(gpsimd):
                        if ob >= 2:
                            gpsimd.wait_ge(sPE.h, peB[ob - 2])
                        gpsimd.dma_start(
                            out=woutb_t[:, (ob % 2) * NCH * 128:(ob % 2 + 1) * NCH * 128],
                            in_=AP(w_out_d, ob * 128 * NCH * 128,
                                   [[NCH * 128, 128], [1, NCH * 128]]),
                        ).then_inc(sW.h, 16)
                        sW.v += 16
                        wout_ready[ob] = sW.v

                if "B" in phases:
                    emit_wout(0)
                    emit_wout(1)

                for ob in (range(16) if "B" in phases else []):
                    @blk.sync
                    def _(sync):
                        if ob >= 2:
                            sync.wait_ge(sEv.h, evB[ob - 2])
                        sync.dma_start(
                            out=xhb_t[:, (ob % 2) * 4 * TH:(ob % 2 + 1) * 4 * TH],
                            in_=AP(xh_spill, ob * 128 * TH,
                                   [[TH, 128], [NCH * 128 * TH, NG], [1, TH]]),
                        ).then_inc(sX.h, 16)
                        sX.v += 16
                        xh_ready[ob] = sX.v

                    @blk.tensor
                    def _(tensor):
                        tensor.wait_ge(sW.h, wout_ready[ob])
                        if ob == 0:
                            tensor.wait_ge(sV.h, aq_done)
                        if ob >= 2:
                            tensor.wait_ge(sEv.h, evB[ob - 2])
                        mm = None
                        base = 0 if ob % 2 == 0 else 4
                        for g in range(NG):
                            for kc in range(NCH):
                                mm = tensor.matmul(
                                    ps[base + g][:, :],
                                    woutb_t[:, ((ob % 2) * NCH + kc) * 128:((ob % 2) * NCH + kc + 1) * 128],
                                    aq_t[:, kc * TC + g * TH:kc * TC + (g + 1) * TH],
                                    start=(kc == 0), stop=(kc == NCH - 1))
                        peB[ob] = sPE.inc(mm)

                    if ob + 2 < 16:
                        emit_wout(ob + 2)

                    @blk.vector
                    def _(vector):
                        vector.wait_ge(sPE.h, peB[ob])
                        vector.wait_ge(sX.h, xh_ready[ob])
                        base = 0 if ob % 2 == 0 else 4
                        a = None
                        for g in range(NG):
                            vector.scalar_tensor_tensor(
                                junkB_t[:, :], ps[base + g][:, :],
                                modt[:, M_GA + ob:M_GA + ob + 1],
                                xhb_t[:, ((ob % 2) * 4 + g) * TH:((ob % 2) * 4 + g + 1) * TH],
                                OP.mult, OP.add)
                            a = vector.tensor_scalar_add(
                                x2_t[:, ob * TC + g * TH:ob * TC + (g + 1) * TH],
                                junkB_t[:, :], gab_t[:, ob:ob + 1])
                        evB[ob] = sEv.inc(a)

                if tap("x2", [128, NCH * TC], BF16):
                    @blk.sync
                    def _(sync):
                        sync.wait_ge(sEv.h, sEv.v)
                        sync.dma_start(out=tap_outs["x2"].ap(),
                                       in_=x2_t[:, :]).then_inc(sO.h, 16)
                        sO.v += 16

            with nc.sbuf_tensor("ht_t", [128, NCH * TC], BF16) as ht_t, \
                 nc.sbuf_tensor("wd128_t", [128, TC], BF16) as wd128_t, \
                 nc.sbuf_tensor("router_t", [128, NCH * 8], BF16) as router_t:

                with nc.sbuf_tensor("h2_t", [128, NCH * TH], F32) as h2_t, \
                     nc.sbuf_tensor("sqC_t", [128, NCH * TH], BF16) as sqC_t, \
                     nc.sbuf_tensor("stC_t", [128, 6 * TH], F32) as stC_t, \
                     nc.sbuf_tensor("lgfm_t", [8, TC], F32) as lgfm_t, \
                     nc.sbuf_tensor("lgtm_t", [128, 2 * 128], F32) as lgtm_t, \
                     nc.sbuf_tensor("wdtm_t", [128, 128], F32) as wdtm_t, \
                     nc.sbuf_tensor("wdtm_bf", [128, 128], BF16) as wdtm_bf, \
                     nc.sbuf_tensor("wdrep_t", [128, 2 * 128], BF16) as wdrep_t, \
                     nc.sbuf_tensor("stR_t", [128, 6 * 16], F32) as stR_t, \
                     nc.Block() as blk:

                    @blk.gpsimd
                    def _(gpsimd):
                        if "C1" not in phases:
                            return
                        gpsimd.dma_start(
                            out=router_t[:, :],
                            in_=AP(router_d, 0, [[NCH * 8, 128], [1, NCH * 8]]),
                        ).then_inc(sW.h, 16)
                        sW.v += 16
                    r_ready = sW.v

                    s_mu = stC_t[:, 0:TH]
                    s_var = stC_t[:, TH:2 * TH]
                    s_rst = stC_t[:, 2 * TH:3 * TH]
                    s_mur = stC_t[:, 3 * TH:4 * TH]
                    s_t2 = stC_t[:, 4 * TH:5 * TH]
                    s_rrm = stC_t[:, 5 * TH:6 * TH]

                    for g in (range(NG) if "C1" in phases else []):
                        x2s = lambda kc, g=g: x2_t[:, kc * TC + g * TH:kc * TC + (g + 1) * TH]

                        @blk.scalar
                        def _(scalar):
                            if g >= 1:
                                scalar.wait_ge(sV.h, sV.v)
                            sSc.inc(scalar.activation(
                                sqC_t[:, :],
                                AP(x2_t, g * TH, [[NCH * TC, 128], [TC, NCH], [1, TH]]),
                                ACT.Square))
                        sc_x2 = sSc.v

                        @blk.tensor
                        def _(tensor):
                            tensor.wait_ge(sSc.h, sc_x2)
                            mm = None
                            for kc in range(NCH):
                                mm = tensor.matmul(ps[0][:, :], ones_t[:, :], x2s(kc),
                                                   start=(kc == 0), stop=(kc == NCH - 1))
                            for kc in range(NCH):
                                mm = tensor.matmul(ps[1][:, :], ones_t[:, :],
                                                   sqC_t[:, kc * TH:(kc + 1) * TH],
                                                   start=(kc == 0), stop=(kc == NCH - 1))
                            sPE.inc(mm)
                        pe_x2 = sPE.v

                        @blk.vector
                        def _(vector):
                            vector.wait_ge(sPE.h, pe_x2)
                            vector.tensor_scalar_mul(s_mu, ps[0][:, :], 1.0 / D)
                            vector.tensor_scalar_mul(s_var, ps[1][:, :], 1.0 / D)
                            vector.tensor_mul(s_t2, s_mu, s_mu)
                            sV.inc(vector.tensor_sub(s_var, s_var, s_t2))
                        v_vx = sV.v

                        @blk.scalar
                        def _(scalar):
                            scalar.wait_ge(sV.h, v_vx)
                            sSc.inc(scalar.activation(s_rst, s_var, ACT.Sqrt,
                                                      bias=epsln_t[:, :]))
                        sc_sx = sSc.v

                        @blk.vector
                        def _(vector):
                            vector.wait_ge(sSc.h, sc_sx)
                            vector.reciprocal(s_rst, s_rst)
                            vector.tensor_mul(s_mur, s_mu, s_rst)
                            a = None
                            for kc in range(NCH):
                                vector.tensor_tensor(s_t2, x2s(kc), s_rst, OP.mult)
                                vector.tensor_sub(s_t2, s_t2, s_mur)
                                a = vector.tensor_scalar(
                                    h2_t[:, kc * TH:(kc + 1) * TH], s_t2,
                                    cb1_t[:, kc:kc + 1], sb_c(kc), OP.mult, OP.add)
                            sV.inc(a)
                        v_h2 = sV.v

                        @blk.scalar
                        def _(scalar):
                            scalar.wait_ge(sV.h, v_h2)
                            sSc.inc(scalar.activation(sqC_t[:, :], h2_t[:, :], ACT.Square))
                        sc_h2 = sSc.v

                        @blk.tensor
                        def _(tensor):
                            tensor.wait_ge(sSc.h, sc_h2)
                            mm = None
                            for kc in range(NCH):
                                mm = tensor.matmul(ps[2][:, :], ones_t[:, :],
                                                   sqC_t[:, kc * TH:(kc + 1) * TH],
                                                   start=(kc == 0), stop=(kc == NCH - 1))
                            sPE.inc(mm)
                        pe_rms = sPE.v

                        @blk.vector
                        def _(vector):
                            vector.wait_ge(sPE.h, pe_rms)
                            sV.inc(vector.tensor_scalar_mul(s_t2, ps[2][:, :],
                                                            1.0 / D))
                        v_rms = sV.v

                        @blk.scalar
                        def _(scalar):
                            scalar.wait_ge(sV.h, v_rms)
                            sSc.inc(scalar.activation(s_rrm, s_t2, ACT.Sqrt,
                                                      bias=epsrms_t[:, :]))
                        sc_rms = sSc.v

                        @blk.vector
                        def _(vector):
                            vector.wait_ge(sSc.h, sc_rms)
                            vector.reciprocal(s_rrm, s_rrm)
                            a = None
                            for kc in range(NCH):
                                a = vector.scalar_tensor_tensor(
                                    ht_t[:, kc * TC + g * TH:kc * TC + (g + 1) * TH],
                                    h2_t[:, kc * TH:(kc + 1) * TH],
                                    rmsw_t[:, kc:kc + 1], s_rrm, OP.mult, OP.mult)
                            sV.inc(a)
                    ht_done = sV.v

                    lg_pe = {}
                    lg_ev = {}
                    for g in (range(NG) if "C1" in phases
                              and "noC1b" not in phases else []):
                        @blk.tensor
                        def _(tensor):
                            if g == 0:
                                tensor.wait_ge(sV.h, ht_done)
                                tensor.wait_ge(sW.h, r_ready)
                            if g >= 2:
                                tensor.wait_ge(sEv.h, lg_ev[g - 2])
                            mm = None
                            for kc in range(NCH):
                                mm = tensor.matmul(
                                    ps[3 + (g % 2)][0:8, :],
                                    router_t[:, kc * 8:(kc + 1) * 8],
                                    ht_t[:, kc * TC + g * TH:kc * TC + (g + 1) * TH],
                                    start=(kc == 0), stop=(kc == NCH - 1))
                            lg_pe[g] = sPE.inc(mm)

                        @blk.vector
                        def _(vector):
                            vector.wait_ge(sPE.h, lg_pe[g])
                            lg_ev[g] = sEv.inc(vector.tensor_copy(
                                lgfm_t[0:8, g * TH:(g + 1) * TH],
                                ps[3 + (g % 2)][0:8, :]))
                    lg_done = sEv.v

                    tr_pe = {}
                    tr_ev = {}
                    for s16 in (range(16) if "C1" in phases
                                and "noC1c" not in phases else []):
                        @blk.tensor
                        def _(tensor):
                            if s16 == 0:
                                tensor.wait_ge(sEv.h, lg_done)
                            if s16 >= 2:
                                tensor.wait_ge(sEv.h, tr_ev[s16 - 2])
                            tensor.transpose(
                                ps[5 + (s16 % 2)][0:128, 0:8],
                                lgfm_t[0:8, s16 * 128:(s16 + 1) * 128],
                                ident_t[0:8, 0:8])
                            tr_pe[s16] = sPE.inc(nc.tensor.nop())

                        @blk.vector
                        def _(vector):
                            vector.wait_ge(sPE.h, tr_pe[s16])
                            tr_ev[s16] = sEv.inc(vector.tensor_copy(
                                lgtm_t[:, s16 * 8:(s16 + 1) * 8],
                                ps[5 + (s16 % 2)][0:128, 0:8]))
                    lgtm_done = sEv.v

                    @blk.vector
                    def _(vector):
                        if "C1" not in phases or "noC1d" in phases:
                            return
                        vector.wait_ge(sEv.h, lgtm_done)
                        m1 = stR_t[:, 0:16]; m2 = stR_t[:, 16:32]
                        dl = stR_t[:, 32:48]; w1 = stR_t[:, 48:64]
                        w2 = stR_t[:, 64:80]
                        msk1 = lgtm_t[:, 128:256]
                        vector.engine_nop()
                        vector.engine_nop()
                        vector.reduce_max(m1, AP(lgtm_t, 0, [[256, 128], [8, 16], [1, 8]]),
                                          axis=AX.X)
                        vector.engine_nop()
                        vector.engine_nop()
                        vector.tensor_tensor(
                            msk1, lgtm_t[:, 0:128],
                            AP(stR_t, 0, [[96, 128], [1, 16], [0, 8]]), OP.is_equal)
                        vector.engine_nop()
                        vector.engine_nop()
                        vector.scalar_tensor_tensor(
                            wdtm_t[:, :], msk1, -1e30, lgtm_t[:, 0:128],
                            OP.mult, OP.add)
                        vector.engine_nop()
                        vector.engine_nop()
                        vector.reduce_max(m2, AP(wdtm_t, 0, [[128, 128], [8, 16], [1, 8]]),
                                          axis=AX.X)
                        vector.engine_nop()
                        vector.engine_nop()
                        vector.tensor_tensor(
                            wdtm_t[:, :], wdtm_t[:, :],
                            AP(stR_t, 16, [[96, 128], [1, 16], [0, 8]]), OP.is_equal)
                        sV.inc(vector.tensor_sub(dl, m1, m2))
                    v_top = sV.v

                    @blk.scalar
                    def _(scalar):
                        if "C1" not in phases or "noC1d" in phases:
                            return
                        scalar.wait_ge(sV.h, v_top)
                        sSc.inc(scalar.activation(stR_t[:, 48:64], stR_t[:, 32:48],
                                                  ACT.Tanh, scale=0.5))
                    sc_tanh = sSc.v

                    @blk.vector
                    def _(vector):
                        if "C1" not in phases or "noC1d" in phases:
                            return
                        vector.wait_ge(sSc.h, sc_tanh)
                        w1 = stR_t[:, 48:64]; w2 = stR_t[:, 64:80]
                        vector.tensor_scalar(w1, w1, 0.5, 0.5, OP.mult, OP.add)
                        vector.engine_nop()
                        vector.engine_nop()
                        vector.tensor_scalar(w2, w1, -1.0, 1.0, OP.mult, OP.add)
                        vector.engine_nop()
                        vector.engine_nop()
                        vector.tensor_tensor(
                            lgtm_t[:, 128:256], lgtm_t[:, 128:256],
                            AP(stR_t, 48, [[96, 128], [1, 16], [0, 8]]), OP.mult)
                        vector.tensor_tensor(
                            wdtm_t[:, :], wdtm_t[:, :],
                            AP(stR_t, 64, [[96, 128], [1, 16], [0, 8]]), OP.mult)
                        vector.engine_nop()
                        vector.engine_nop()
                        sV.inc(vector.tensor_add(wdtm_t[:, :], wdtm_t[:, :],
                                                 lgtm_t[:, 128:256]))
                    wd_done = sV.v

                    if tap("wdtm", [128, 128]):
                        @blk.sync
                        def _(sync):
                            sync.wait_ge(sV.h, wd_done)
                            sync.dma_start(out=tap_outs["wdtm"].ap(),
                                           in_=wdtm_t[:, :]).then_inc(sO.h, 16)
                            sO.v += 16

                    @blk.scalar
                    def _(scalar):
                        if "C1" not in phases or "noC1e" in phases:
                            return
                        scalar.wait_ge(sV.h, wd_done)
                        sSc.inc(scalar.copy(wdtm_bf[:, :], wdtm_t[:, :]))
                    wdbf_done = sSc.v

                    bt_rep = {}
                    bt_pe = {}
                    bt_ev = {}
                    for s16 in (range(16) if "C1" in phases
                                and "noC1e" not in phases else []):
                        @blk.vector
                        def _(vector):
                            if s16 == 0:
                                vector.wait_ge(sSc.h, wdbf_done)
                            if s16 >= 2:
                                vector.wait_ge(sPE.h, bt_pe[s16 - 2])
                            bt_rep[s16] = sV.inc(vector.tensor_copy(
                                wdrep_t[:, (s16 % 2) * 128:(s16 % 2 + 1) * 128],
                                AP(wdtm_bf, s16 * 8,
                                   [[128, 128], [0, 16], [1, 8]])))

                        @blk.tensor
                        def _(tensor):
                            tensor.wait_ge(sV.h, bt_rep[s16])
                            if s16 >= 2:
                                tensor.wait_ge(sEv.h, bt_ev[s16 - 2])
                            mm = tensor.matmul(
                                ps[5 + (s16 % 2)][0:128, 0:128],
                                wdrep_t[:, (s16 % 2) * 128:(s16 % 2 + 1) * 128],
                                identbf_t[:, :], start=True, stop=True)
                            bt_pe[s16] = sPE.inc(mm)

                        @blk.vector
                        def _(vector):
                            vector.wait_ge(sPE.h, bt_pe[s16])
                            bt_ev[s16] = sEv.inc(vector.tensor_copy(
                                wd128_t[:, s16 * 128:(s16 + 1) * 128],
                                ps[5 + (s16 % 2)][0:128, 0:128]))
                    wdfm_done = sEv.v

                with nc.sbuf_tensor("m_t", [128, NFB * TC], BF16) as m_t, \
                     nc.sbuf_tensor("wgb_t", [128, 2 * NCH * 128], BF16) as wgb_t, \
                     nc.sbuf_tensor("wub_t", [128, 2 * NCH * 128], BF16) as wub_t, \
                     nc.sbuf_tensor("wdb_t", [128, 2 * NFB * 128], BF16) as wdb_t, \
                     nc.sbuf_tensor("sg_t", [128, TC], BF16) as sg_t, \
                     nc.sbuf_tensor("sgf_t", [128, 2 * TH], F32) as sgf_t, \
                     nc.sbuf_tensor("wvC_t", [128, TC], F32) as wvC_t, \
                     nc.sbuf_tensor("junkC", [128, TH], F32) as junkC_t, \
                     nc.Block() as blk:

                    experts = ["sh"] + list(range(E))
                    wg_rdy = {}
                    wu_rdy = {}
                    wd_rdy = {}
                    pe_g = {}
                    pe_u = {}
                    sc_sg = {}
                    ev_m = {}
                    pe_z = {}
                    ev_z = {}
                    wv_rdy = {}
                    last_ev_z = [0]
                    last_pe_z_all = {}

                    def gdma(eix, fb):
                        e = experts[eix]

                        @blk.gpsimd
                        def _(gpsimd):
                            slot = (eix * NFB + fb) % 2
                            if eix * NFB + fb >= 2:
                                prev = (eix * NFB + fb) - 2
                                pe_ix, pfb = divmod(prev, NFB)
                                gpsimd.wait_ge(sPE.h, pe_u[(pe_ix, pfb)])
                            for dram, buf, rdy in ((wg_d if e != "sh" else swg_d, wgb_t, wg_rdy),
                                                   (wu_d if e != "sh" else swu_d, wub_t, wu_rdy)):
                                off = ((e * NFB + fb) if e != "sh" else fb) * 128 * NCH * 128
                                gpsimd.dma_start(
                                    out=buf[:, slot * NCH * 128:(slot + 1) * NCH * 128],
                                    in_=AP(dram, off, [[NCH * 128, 128], [1, NCH * 128]]),
                                ).then_inc(sW.h, 16)
                                sW.v += 16
                                rdy[(eix, fb)] = sW.v

                    def ddma(eix, db):
                        e = experts[eix]

                        @blk.gpsimd
                        def _(gpsimd):
                            slot = (eix * 16 + db) % 2
                            if eix * 16 + db >= 2:
                                prev = (eix * 16 + db) - 2
                                pe_ix, pdb = divmod(prev, 16)
                                gpsimd.wait_ge(sPE.h, last_pe_z_all[(pe_ix, pdb)])
                            off = ((e * 16 + db) if e != "sh" else db) * 128 * NFB * 128
                            gpsimd.dma_start(
                                out=wdb_t[:, slot * NFB * 128:(slot + 1) * NFB * 128],
                                in_=AP(wd_d if e != "sh" else swd_d, off,
                                       [[NFB * 128, 128], [1, NFB * 128]]),
                            ).then_inc(sW.h, 16)
                            sW.v += 16
                            wd_rdy[(eix, db)] = sW.v

                    if "C2" in phases:
                        gdma(0, 0)
                        gdma(0, 1)

                    last_ev_z_all = {}

                    for eix, e in (list(enumerate(experts)) if "C2" in phases else []):
                        if e != "sh" and "noC2wv" in phases:
                            wv_rdy[eix] = sEv.v
                        elif e != "sh":
                            wv_pe = {}
                            wv_ev = {}
                            for g in range(NG):
                                @blk.tensor
                                def _(tensor):
                                    if g == 0:
                                        tensor.wait_ge(sEv.h, ev_m[(eix - 1, NFB - 1)])
                                    else:
                                        tensor.wait_ge(sEv.h, wv_ev[g - 1])
                                    mm = tensor.matmul(
                                        ps[6][:, :],
                                        sel16_t[:, e * 128:(e + 1) * 128],
                                        wd128_t[:, g * TH:(g + 1) * TH],
                                        start=True, stop=True)
                                    wv_pe[g] = sPE.inc(mm)

                                @blk.vector
                                def _(vector):
                                    vector.wait_ge(sPE.h, wv_pe[g])
                                    wv_ev[g] = sEv.inc(vector.tensor_copy(
                                        wvC_t[:, g * TH:(g + 1) * TH], ps[6][:, :]))
                            wv_rdy[eix] = sEv.v

                        for fb in range(NFB):
                            slot = (eix * NFB + fb) % 2

                            @blk.tensor
                            def _(tensor):
                                tensor.wait_ge(sW.h, wg_rdy[(eix, fb)])
                                if eix > 0 and fb == 0:
                                    tensor.wait_ge(sPE.h, last_pe_z_all[(eix - 1, 15)])
                                    tensor.wait_ge(sEv.h, last_ev_z_all[eix - 1])
                                if (eix, fb) != (0, 0):
                                    prev = (eix * NFB + fb) - 1
                                    pix, pfb = divmod(prev, NFB)
                                    tensor.wait_ge(sSc.h, sc_sg[(pix, pfb)])
                                mm = None
                                for kc in range(NCH):
                                    lhs = wgb_t[:, (slot * NCH + kc) * 128:(slot * NCH + kc + 1) * 128]
                                    for g in range(NG):
                                        mm = tensor.matmul(
                                            ps[g][:, :], lhs,
                                            ht_t[:, kc * TC + g * TH:kc * TC + (g + 1) * TH],
                                            start=(kc == 0), stop=(kc == NCH - 1))
                                pe_g[(eix, fb)] = sPE.inc(mm)

                            @blk.scalar
                            def _(scalar):
                                scalar.wait_ge(sPE.h, pe_g[(eix, fb)])
                                if (eix, fb) != (0, 0):
                                    prev = (eix * NFB + fb) - 1
                                    pix, pfb = divmod(prev, NFB)
                                    scalar.wait_ge(sEv.h, ev_m[(pix, pfb)])
                                a = None
                                for g in range(NG):
                                    scalar.activation(
                                        sgf_t[:, (g % 2) * TH:(g % 2 + 1) * TH],
                                        ps[g][:, :], ACT.Identity,
                                        bias=zcol_t[:, :], scale=1.0)
                                    if g >= 1:
                                        scalar.activation(
                                            sg_t[:, (g - 1) * TH:g * TH],
                                            sgf_t[:, ((g - 1) % 2) * TH:((g - 1) % 2 + 1) * TH],
                                            ACT.Silu)
                                a = scalar.activation(
                                    sg_t[:, 3 * TH:4 * TH],
                                    sgf_t[:, TH:2 * TH], ACT.Silu)
                                sc_sg[(eix, fb)] = sSc.inc(a)

                            @blk.tensor
                            def _(tensor):
                                tensor.wait_ge(sW.h, wu_rdy[(eix, fb)])
                                tensor.wait_ge(sSc.h, sc_sg[(eix, fb)])
                                if (eix, fb) != (0, 0):
                                    prev = (eix * NFB + fb) - 1
                                    pix, pfb = divmod(prev, NFB)
                                    tensor.wait_ge(sEv.h, ev_m[(pix, pfb)])
                                if e != "sh" and fb == 0:
                                    tensor.wait_ge(sEv.h, wv_rdy[eix])
                                mm = None
                                for kc in range(NCH):
                                    lhs = wub_t[:, (slot * NCH + kc) * 128:(slot * NCH + kc + 1) * 128]
                                    for g in range(NG):
                                        mm = tensor.matmul(
                                            ps[4 + g][:, :], lhs,
                                            ht_t[:, kc * TC + g * TH:kc * TC + (g + 1) * TH],
                                            start=(kc == 0), stop=(kc == NCH - 1))
                                pe_u[(eix, fb)] = sPE.inc(mm)

                            if eix * NFB + fb + 2 < len(experts) * NFB:
                                nxt = eix * NFB + fb + 2
                                gdma(nxt // NFB, nxt % NFB)

                            @blk.vector
                            def _(vector):
                                vector.wait_ge(sPE.h, pe_u[(eix, fb)])
                                if e != "sh":
                                    vector.wait_ge(sEv.h, wv_rdy[eix])
                                a = None
                                for g in range(NG):
                                    dst = m_t[:, fb * TC + g * TH:fb * TC + (g + 1) * TH]
                                    if e == "sh" or "noC2wv" in phases:
                                        a = vector.tensor_tensor(
                                            dst, ps[4 + g][:, :],
                                            sg_t[:, g * TH:(g + 1) * TH], OP.mult)
                                    else:
                                        vector.tensor_tensor(
                                            junkC_t[:, :], ps[4 + g][:, :],
                                            sg_t[:, g * TH:(g + 1) * TH], OP.mult)
                                        a = vector.tensor_tensor(
                                            dst, junkC_t[:, :],
                                            wvC_t[:, g * TH:(g + 1) * TH], OP.mult)
                                ev_m[(eix, fb)] = sEv.inc(a)

                        if "noC2dn" in phases:
                            for db in range(16):
                                last_pe_z_all[(eix, db)] = sPE.v
                            last_ev_z_all[eix] = sEv.v
                            continue
                        ddma(eix, 0)
                        ddma(eix, 1)
                        for db in range(16):
                            slot = (eix * 16 + db) % 2

                            @blk.tensor
                            def _(tensor):
                                tensor.wait_ge(sW.h, wd_rdy[(eix, db)])
                                if db == 0:
                                    tensor.wait_ge(sEv.h, ev_m[(eix, NFB - 1)])
                                    tensor.wait_ge(sSc.h, sc_sg[(eix, NFB - 1)])
                                for g in range(NG):
                                    if db >= 1:
                                        tensor.wait_ge(sEv.h, ev_z[(eix, db - 1, g)])
                                    mm = None
                                    for kc in range(NFB):
                                        mm = tensor.matmul(
                                            ps[g][:, :],
                                            wdb_t[:, (slot * NFB + kc) * 128:(slot * NFB + kc + 1) * 128],
                                            m_t[:, kc * TC + g * TH:kc * TC + (g + 1) * TH],
                                            start=(kc == 0), stop=(kc == NFB - 1))
                                    pe_z[(eix, db, g)] = sPE.inc(mm)

                            @blk.vector
                            def _(vector):
                                for g in range(NG):
                                    vector.wait_ge(sPE.h, pe_z[(eix, db, g)])
                                    ev_z[(eix, db, g)] = sEv.inc(
                                        vector.scalar_tensor_tensor(
                                            x2_t[:, db * TC + g * TH:db * TC + (g + 1) * TH],
                                            ps[g][:, :], gb_c(db),
                                            x2_t[:, db * TC + g * TH:db * TC + (g + 1) * TH],
                                            OP.mult, OP.add))
                            last_pe_z_all[(eix, db)] = pe_z[(eix, db, NG - 1)]
                            if db + 2 < 16:
                                ddma(eix, db + 2)
                        last_ev_z_all[eix] = ev_z[(eix, 15, NG - 1)]

                    @blk.gpsimd
                    def _(gpsimd):
                        if "C2" not in phases:
                            return
                        gpsimd.wait_ge(sEv.h, sEv.v)
                        for kc in range(NCH):
                            gpsimd.dma_start(
                                out=AP(out_fm, kc * 128 * TC, [[TC, 128], [1, TC]]),
                                in_=x2_t[:, kc * TC:(kc + 1) * TC]).then_inc(sO.h, 16)
                            sO.v += 16
                        gpsimd.wait_ge(sO.h, sO.v)

        with nc.Block() as blk:
            @blk.gpsimd
            def _(gpsimd):
                for sem in (sW, sX, sSp, sCC, sO):
                    if sem.v:
                        gpsimd.wait_ge(sem.h, sem.v)

    return nc, tap_outs


def kernel(**inputs):
    in_maps = make_host_inputs(inputs)
    nc, _ = build_program()
    res = run_bass_kernel_spmd(nc, in_maps, core_ids=list(range(8)))
    return assemble_output(res.results)


# version 18
# speedup vs baseline: 1.1023x; 1.1023x over previous
import contextlib

import numpy as np
import ml_dtypes

import concourse.bass as bass
import concourse.mybir as mybir
from concourse.bass_utils import run_bass_kernel_spmd

F32 = mybir.dt.float32
BF16 = mybir.dt.bfloat16
OP = mybir.AluOpType
ACT = mybir.ActivationFunctionType
AX = mybir.AxisListType
BF = ml_dtypes.bfloat16
AP = bass.AP

B, S, D = 4, 4096, 2048
NCH = 16
TC = 2048
NTAU = 2
TT = 1024
TH = 512
NG = 4
NBLK = 48
E, F = 8, 1024
NFB = 8
LN_EPS = 1e-5
RMS_EPS = 1e-6

PERM = np.concatenate([np.arange(0, 1024, 2), np.arange(1, 1024, 2),
                       np.arange(1024, 2048)]).astype(np.int64)


def _win_col_order():
    cols, kinds = [], []
    for q in range(4):
        cols.append(2048 + PERM[q * 128:(q + 1) * 128]); kinds.append(("KE", q))
        cols.append(2048 + PERM[512 + q * 128:512 + (q + 1) * 128]); kinds.append(("KO", q))
        cols.append(4096 + PERM[q * 128:(q + 1) * 128]); kinds.append(("VE", q))
        cols.append(4096 + PERM[512 + q * 128:512 + (q + 1) * 128]); kinds.append(("VO", q))
    for j in range(8):
        cols.append(2048 + PERM[1024 + j * 128:1024 + (j + 1) * 128]); kinds.append(("KP", j))
        cols.append(4096 + PERM[1024 + j * 128:1024 + (j + 1) * 128]); kinds.append(("VP", j))
    for i in range(16):
        cols.append(PERM[i * 128:(i + 1) * 128]); kinds.append(("Q", i))
    return np.concatenate(cols), kinds


WIN_COLS, WIN_KINDS = _win_col_order()


def _blocked_lhs(w, n_out_blocks):
    K, M = w.shape
    return np.ascontiguousarray(
        w.reshape(K // 128, 128, n_out_blocks, 128).transpose(2, 1, 0, 3))


def make_host_inputs(inp):
    f32 = lambda k: np.ascontiguousarray(np.asarray(inp[k]), dtype=np.float32)
    X = f32("L0_attn_X")
    W_in = f32("L0_attn_W_in"); b_in = f32("L0_attn_b_in")
    W_out = f32("L0_attn_W_out"); b_out = f32("L0_attn_b_out")
    gamma = f32("L0_attn_gamma"); beta = f32("L0_attn_beta")
    rmsw = f32("L0_ffn_rms_weight"); router_w = f32("L0_ffn_router_w")
    w_gate = f32("L0_ffn_w_gate"); w_up = f32("L0_ffn_w_up")
    w_down = f32("L0_ffn_w_down")
    sw_gate = f32("L0_ffn_sw_gate"); sw_up = f32("L0_ffn_sw_up")
    sw_down = f32("L0_ffn_sw_down")
    c_mod = f32("L0_c"); wmod = f32("L0_wmod")

    sh = {}
    sh["w_in_blk"] = _blocked_lhs(W_in[:, WIN_COLS], NBLK).astype(BF)
    sh["w_out_blk"] = _blocked_lhs(W_out[PERM, :], 16).astype(BF)
    sh["wg_blk"] = np.stack([_blocked_lhs(w_gate[e], NFB) for e in range(E)]).astype(BF)
    sh["wu_blk"] = np.stack([_blocked_lhs(w_up[e], NFB) for e in range(E)]).astype(BF)
    sh["wd_blk"] = np.stack([_blocked_lhs(w_down[e], 16) for e in range(E)]).astype(BF)
    sh["swg_blk"] = _blocked_lhs(sw_gate, NFB).astype(BF)
    sh["swu_blk"] = _blocked_lhs(sw_up, NFB).astype(BF)
    sh["swd_blk"] = _blocked_lhs(sw_down, 16).astype(BF)
    sh["router_blk"] = np.ascontiguousarray(
        router_w.reshape(NCH, 128, 8).transpose(1, 0, 2)).astype(BF)
    sh["b_qkv"] = np.ascontiguousarray(b_in[WIN_COLS].reshape(NBLK, 128).T)
    for name, v in (("gamma_t", gamma), ("beta_t", beta),
                    ("rmsw_t", rmsw), ("b_out_t", b_out)):
        sh[name] = np.ascontiguousarray(v.reshape(NCH, 128).T)
    sh["ones_bf"] = np.ones((128, 128), dtype=BF)
    sel16 = np.zeros((128, 8 * 128), dtype=BF)
    for e in range(8):
        sel16[np.arange(e, 128, 8), e * 128:(e + 1) * 128] = BF(1.0 / 16.0)
    sh["sel16_bf"] = sel16
    sh["ones1_bf"] = np.ones((1, 128), dtype=BF)
    sh["ident"] = np.eye(128, dtype=np.float32)
    sh["ident_bf"] = np.eye(128, dtype=BF)

    inv_freq = 1.0 / (10000.0 ** (np.arange(0, 1024, 2, dtype=np.float64) / 1024.0))
    ang = np.arange(S, dtype=np.float64)[None, :] * inv_freq[:, None]
    cos_t = np.cos(ang).astype(np.float32)
    sin_t = np.sin(ang).astype(np.float32)

    in_maps = []
    for core in range(8):
        b = core // 2
        sl = slice((core % 2) * TC, (core % 2) * TC + TC)
        m = dict(sh)
        m["x_fm"] = np.ascontiguousarray(X[b].T.reshape(NCH, 128, S)[:, :, sl])
        m["cos_sh"] = np.ascontiguousarray(cos_t[:, sl].reshape(4, 128, TC))
        m["sin_sh"] = np.ascontiguousarray(sin_t[:, sl].reshape(4, 128, TC))
        m["wmod_blk"] = np.ascontiguousarray(
            wmod[core * 256:(core + 1) * 256, :]
            .reshape(2, 128, 96, 128).transpose(1, 0, 2, 3)).astype(BF)
        m["c_slice"] = np.ascontiguousarray(
            c_mod[:, core * 256:(core + 1) * 256].T.reshape(2, 128, 4))
        selv = np.zeros((128, 4), dtype=np.float32)
        selv[:, b] = 1.0
        m["sel4"] = selv
        in_maps.append(m)
    return in_maps


def assemble_output(results):
    out = np.empty((B, S, D), dtype=np.float32)
    for core in range(8):
        b, shh = core // 2, core % 2
        o = np.asarray(results[core]["out_fm"])
        out[b, shh * TC:(shh + 1) * TC, :] = o.reshape(D, TC).T
    return out


class Sem:
    def __init__(self, es, nc, name):
        self.h = es.enter_context(nc.semaphore(name))
        self.v = 0

    def inc(self, instr, n=1):
        instr.then_inc(self.h, n)
        self.v += n
        return self.v


def build_program(taps=(), phases=("P0", "A", "A2", "AF", "B", "C1", "C2")):
    phases = set(phases)
    nc = bass.Bass("TRN2", target_bir_lowering=False,
                   detect_race_conditions=False)
    dt = nc.dram_tensor

    x_fm = dt("x_fm", [NCH, 128, TC], F32, kind="ExternalInput")
    w_in_d = dt("w_in_blk", [NBLK, 128, NCH, 128], BF16, kind="ExternalInput")
    w_out_d = dt("w_out_blk", [16, 128, NCH, 128], BF16, kind="ExternalInput")
    wg_d = dt("wg_blk", [E, NFB, 128, NCH, 128], BF16, kind="ExternalInput")
    wu_d = dt("wu_blk", [E, NFB, 128, NCH, 128], BF16, kind="ExternalInput")
    wd_d = dt("wd_blk", [E, 16, 128, NFB, 128], BF16, kind="ExternalInput")
    swg_d = dt("swg_blk", [NFB, 128, NCH, 128], BF16, kind="ExternalInput")
    swu_d = dt("swu_blk", [NFB, 128, NCH, 128], BF16, kind="ExternalInput")
    swd_d = dt("swd_blk", [16, 128, NFB, 128], BF16, kind="ExternalInput")
    router_d = dt("router_blk", [128, NCH, 8], BF16, kind="ExternalInput")
    bqkv_d = dt("b_qkv", [128, NBLK], F32, kind="ExternalInput")
    gamma_d = dt("gamma_t", [128, NCH], F32, kind="ExternalInput")
    beta_d = dt("beta_t", [128, NCH], F32, kind="ExternalInput")
    rmsw_d = dt("rmsw_t", [128, NCH], F32, kind="ExternalInput")
    bout_d = dt("b_out_t", [128, NCH], F32, kind="ExternalInput")
    ones_d = dt("ones_bf", [128, 128], BF16, kind="ExternalInput")
    ones1_d = dt("ones1_bf", [1, 128], BF16, kind="ExternalInput")
    sel16_d = dt("sel16_bf", [128, 8 * 128], BF16, kind="ExternalInput")
    ident_d = dt("ident", [128, 128], F32, kind="ExternalInput")
    identbf_d = dt("ident_bf", [128, 128], BF16, kind="ExternalInput")
    cos_d = dt("cos_sh", [4, 128, TC], F32, kind="ExternalInput")
    sin_d = dt("sin_sh", [4, 128, TC], F32, kind="ExternalInput")
    wmod_d = dt("wmod_blk", [128, 2, 96, 128], BF16, kind="ExternalInput")
    c_d = dt("c_slice", [2, 128, 4], F32, kind="ExternalInput")
    sel_d = dt("sel4", [128, 4], F32, kind="ExternalInput")

    out_fm = dt("out_fm", [NCH, 128, TC], F32, kind="ExternalOutput")

    q_spill = dt("q_spill", [NTAU, 16, 2, 128, TH], BF16)
    xh_spill = dt("xh_spill", [NG, NCH, 128, TH], BF16)
    cc_mod_in = dt("cc_mod_in", [128, 384], F32)
    cc_mod_out = dt("cc_mod_out", [128, 384], F32)
    cc_a_in = dt("cc_a_in", [128, NCH], F32)
    cc_a_out = dt("cc_a_out", [128, NCH], F32)

    tap_outs = {}

    def tap(name, shape, dtype=F32):
        if name in taps and name not in tap_outs:
            tap_outs[name] = dt("tap_" + name, shape, dtype, kind="ExternalOutput")
        return tap_outs.get(name)

    es = contextlib.ExitStack()
    with es:
        def sbuf(name, p, n, dtype):
            return es.enter_context(nc.sbuf_tensor(name, [p, n], dtype))

        ones_t = sbuf("ones_t", 128, 128, BF16)
        ones1_t = sbuf("ones1_t", 1, 128, BF16)
        sel16_t = sbuf("sel16_t", 128, 8 * 128, BF16)
        ident_t = sbuf("ident_t", 128, 128, F32)
        identbf_t = sbuf("identbf_t", 128, 128, BF16)
        bqkv_t = sbuf("bqkv_t", 128, NBLK, F32)
        gamma_t = sbuf("gamma_sb", 128, NCH, F32)
        beta_t = sbuf("beta_sb", 128, NCH, F32)
        rmsw_t = sbuf("rmsw_sb", 128, NCH, F32)
        bout_t = sbuf("bout_sb", 128, NCH, F32)
        modt = sbuf("modt", 128, 96, F32)
        ca1_t = sbuf("ca1_t", 128, NCH, F32)
        cb1_t = sbuf("cb1_t", 128, NCH, F32)
        gab_t = sbuf("gab_t", 128, NCH, F32)
        acol_t = sbuf("acol_t", 128, NCH * NTAU, F32)
        afin_t = sbuf("afin_t", 128, NCH, F32)
        epsln_t = sbuf("epsln_t", 128, 1, F32)
        epsrms_t = sbuf("epsrms_t", 128, 1, F32)
        zcol_t = sbuf("zcol_t", 128, 1, F32)

        sW = Sem(es, nc, "sW")
        sX = Sem(es, nc, "sX")
        sSp = Sem(es, nc, "sSp")
        sPE = Sem(es, nc, "sPE")
        sV = Sem(es, nc, "sV")
        sSc = Sem(es, nc, "sSc")
        sCC = Sem(es, nc, "sCC")
        sEv = Sem(es, nc, "sEv")
        sO = Sem(es, nc, "sO")

        ps = [es.enter_context(nc.psum_tensor(f"ps{i}", [128, TH], F32))
              for i in range(8)]

        M_SA, M_CA, M_GA, M_SB, M_CB, M_GB = 0, 16, 32, 48, 64, 80
        sa_c = lambda kc: modt[:, M_SA + kc:M_SA + kc + 1]
        ga_c = lambda kc: modt[:, M_GA + kc:M_GA + kc + 1]
        sb_c = lambda kc: modt[:, M_SB + kc:M_SB + kc + 1]
        gb_c = lambda kc: modt[:, M_GB + kc:M_GB + kc + 1]

        with nc.sbuf_tensor("wmod_t", [128, 2 * 96 * 128], BF16) as wmod_t, \
             nc.sbuf_tensor("c_t", [128, 8], F32) as c_t, \
             nc.sbuf_tensor("cs_t", [128, 8], BF16) as cs_t, \
             nc.sbuf_tensor("sel_t", [128, 4], F32) as sel_t, \
             nc.sbuf_tensor("modev", [128, 384], F32) as modev_t, \
             nc.sbuf_tensor("modrd", [128, 384], F32) as modrd_t, \
             nc.Block() as blk:

            @blk.sync
            def _(sync):
                for dst, src in ((ones_t, ones_d), (ones1_t, ones1_d),
                                 (sel16_t, sel16_d),
                                 (ident_t, ident_d), (identbf_t, identbf_d),
                                 (bqkv_t, bqkv_d),
                                 (gamma_t, gamma_d), (beta_t, beta_d),
                                 (rmsw_t, rmsw_d), (bout_t, bout_d),
                                 (sel_t, sel_d)):
                    sync.dma_start(out=dst[:, :], in_=src.ap()).then_inc(sX.h, 16)
                    sX.v += 16
                sync.dma_start(out=c_t[:, :],
                               in_=AP(c_d, 0, [[4, 128], [512, 2], [1, 4]])
                               ).then_inc(sX.h, 16)
                sX.v += 16
            c_ready = sX.v

            @blk.gpsimd
            def _(gpsimd):
                gpsimd.memset(epsln_t[:, :], LN_EPS)
                gpsimd.memset(epsrms_t[:, :], RMS_EPS)
                gpsimd.memset(zcol_t[:, :], 0.0)
                gpsimd.dma_start(
                    out=wmod_t[:, :],
                    in_=AP(wmod_d, 0, [[2 * 96 * 128, 128], [1, 2 * 96 * 128]]),
                ).then_inc(sW.h, 16)
                sW.v += 16
            w_ready0 = sW.v

            @blk.scalar
            def _(scalar):
                scalar.wait_ge(sX.h, c_ready)
                sSc.inc(scalar.activation(cs_t[:, :], c_t[:, :], ACT.Silu))
            silu_done = sSc.v

            @blk.tensor
            def _(tensor):
                tensor.wait_ge(sSc.h, silu_done)
                tensor.wait_ge(sW.h, w_ready0)
                mm = None
                for jb in range(96):
                    for kc in range(2):
                        mm = tensor.matmul(
                            ps[0][:, jb * 4:(jb + 1) * 4],
                            wmod_t[:, (kc * 96 + jb) * 128:(kc * 96 + jb + 1) * 128],
                            cs_t[:, kc * 4:(kc + 1) * 4],
                            start=(kc == 0), stop=(kc == 1))
                sPE.inc(mm)
            mod_pe = sPE.v

            @blk.vector
            def _(vector):
                vector.wait_ge(sPE.h, mod_pe)
                sV.inc(vector.tensor_copy(modev_t[:, :], ps[0][:, 0:384]))
            mod_ev = sV.v

            @blk.gpsimd
            def _(gpsimd):
                gpsimd.wait_ge(sV.h, mod_ev)
                gpsimd.dma_start(out=cc_mod_in.ap(), in_=modev_t[:, :]).then_inc(sW.h, 16)
                sW.v += 16
                gpsimd.wait_ge(sW.h, sW.v)
                sCC.inc(gpsimd.collective_compute(
                    "AllReduce", OP.add, replica_groups=[list(range(8))],
                    ins=[cc_mod_in.ap()], outs=[cc_mod_out.ap()]))
                gpsimd.wait_ge(sCC.h, sCC.v)
                gpsimd.dma_start(out=modrd_t[:, :], in_=cc_mod_out.ap()).then_inc(sW.h, 16)
                sW.v += 16
            mod_back = sW.v

            @blk.vector
            def _(vector):
                vector.wait_ge(sW.h, mod_back)
                vector.tensor_tensor(
                    modrd_t[:, :], modrd_t[:, :],
                    AP(sel_t, 0, [[4, 128], [0, 96], [1, 4]]), OP.mult)
                vector.reduce_sum(
                    modt[:, :], AP(modrd_t, 0, [[384, 128], [4, 96], [1, 4]]),
                    axis=AX.X)
                gA = modt[:, M_GA:M_GA + 16]
                gB = modt[:, M_GB:M_GB + 16]
                t1A = modrd_t[:, 0:16]; t2A = modrd_t[:, 16:32]
                t3A = modrd_t[:, 32:48]
                t1B = modrd_t[:, 48:64]; t2B = modrd_t[:, 64:80]
                t3B = modrd_t[:, 80:96]
                vector.engine_nop()
                vector.engine_nop()
                vector.tensor_single_scalar(t3A, gA, 0.0, OP.is_ge)
                vector.tensor_single_scalar(t3B, gB, 0.0, OP.is_ge)
                vector.tensor_scalar_max(t1A, gA, 0.25)
                vector.tensor_scalar_max(t1B, gB, 0.25)
                vector.tensor_scalar_min(t2A, gA, -0.25)
                vector.tensor_scalar_min(t2B, gB, -0.25)
                vector.tensor_scalar(t3A, t3A, 0.5, -0.25, OP.mult, OP.add)
                vector.tensor_scalar(t3B, t3B, 0.5, -0.25, OP.mult, OP.add)
                vector.tensor_add(t1A, t1A, t2A)
                vector.tensor_add(t1B, t1B, t2B)
                vector.tensor_scalar_add(ca1_t[:, :], modt[:, M_CA:M_CA + 16],
                                         1.0)
                vector.tensor_add(gA, t1A, t3A)
                vector.tensor_scalar_add(cb1_t[:, :], modt[:, M_CB:M_CB + 16],
                                         1.0)
                vector.tensor_add(gB, t1B, t3B)
                vector.engine_nop()
                vector.engine_nop()
                sV.inc(vector.tensor_mul(gab_t[:, :], modt[:, M_GA:M_GA + 16],
                                         bout_t[:, :]))
            mod_done = sV.v

            if tap("modt", [128, 96]):
                @blk.sync
                def _(sync):
                    sync.wait_ge(sV.h, mod_done)
                    sync.dma_start(out=tap_outs["modt"].ap(),
                                   in_=modt[:, :]).then_inc(sO.h, 16)
                    sO.v += 16

        with nc.sbuf_tensor("hn_t", [128, NCH * TT], BF16) as hn_t, \
             nc.sbuf_tensor("p_t", [128, NCH * TT], BF16) as p_t:

            for tau in (range(NTAU) if "A" in phases else []):
                with nc.sbuf_tensor(f"x_t{tau}", [128, 2 * NCH * TH], F32) as x_t, \
                     nc.sbuf_tensor(f"h_t{tau}", [128, NCH * TH], BF16) as h_t, \
                     nc.sbuf_tensor(f"xh_t{tau}", [128, NCH * TH], BF16) as xh_t, \
                     nc.sbuf_tensor(f"sq_t{tau}", [128, NCH * TH], BF16) as sq_t, \
                     nc.sbuf_tensor(f"st_t{tau}", [128, 5 * TH], F32) as st_t, \
                     nc.Block() as blk:

                    x_ready = []

                    @blk.sync
                    def _(sync):
                        for st in range(2):
                            g = tau * 2 + st
                            sync.dma_start(
                                out=x_t[:, st * NCH * TH:(st + 1) * NCH * TH],
                                in_=AP(x_fm, g * TH,
                                       [[TC, 128], [128 * TC, NCH], [1, TH]]),
                            ).then_inc(sX.h, 16)
                            sX.v += 16
                            x_ready.append(sX.v)

                    for st in range(2):
                        g = tau * 2 + st
                        xs = lambda kc, st=st: x_t[:, (st * NCH + kc) * TH:(st * NCH + kc + 1) * TH]
                        s_mu = st_t[:, 0:TH]
                        s_var = st_t[:, TH:2 * TH]
                        s_rst = st_t[:, 2 * TH:3 * TH]
                        s_mur = st_t[:, 3 * TH:4 * TH]
                        s_t2 = st_t[:, 4 * TH:5 * TH]
                        hb = 2

                        @blk.scalar
                        def _(scalar):
                            scalar.wait_ge(sX.h, x_ready[st])
                            if st == 1:
                                scalar.wait_ge(sV.h, sV.v)
                            src = x_t[:, st * NCH * TH:(st + 1) * NCH * TH]
                            sSc.inc(scalar.copy(h_t[:, :], src))
                            sSc.inc(scalar.activation(sq_t[:, :], src, ACT.Square))
                        sc_x = sSc.v

                        @blk.tensor
                        def _(tensor):
                            tensor.wait_ge(sSc.h, sc_x)
                            mm = None
                            for kc in range(NCH):
                                mm = tensor.matmul(ps[0][:, :], ones_t[:, :],
                                                   h_t[:, kc * TH:(kc + 1) * TH],
                                                   start=(kc == 0), stop=(kc == NCH - 1))
                            for kc in range(NCH):
                                mm = tensor.matmul(ps[1][:, :], ones_t[:, :],
                                                   sq_t[:, kc * TH:(kc + 1) * TH],
                                                   start=(kc == 0), stop=(kc == NCH - 1))
                            sPE.inc(mm)
                        pe_st = sPE.v

                        @blk.vector
                        def _(vector):
                            vector.wait_ge(sPE.h, pe_st)
                            vector.tensor_scalar_mul(s_mu, ps[0][:, :], 1.0 / D)
                            vector.tensor_scalar_mul(s_var, ps[1][:, :], 1.0 / D)
                            vector.tensor_mul(s_t2, s_mu, s_mu)
                            sV.inc(vector.tensor_sub(s_var, s_var, s_t2))
                        v_var = sV.v

                        @blk.scalar
                        def _(scalar):
                            scalar.wait_ge(sV.h, v_var)
                            sSc.inc(scalar.activation(s_rst, s_var, ACT.Sqrt,
                                                      bias=epsln_t[:, :]))
                        sc_std = sSc.v

                        @blk.vector
                        def _(vector):
                            vector.wait_ge(sSc.h, sc_std)
                            vector.reciprocal(s_rst, s_rst)
                            vector.tensor_mul(s_mur, s_mu, s_rst)
                            for kc in range(NCH):
                                vector.tensor_tensor(s_t2, xs(kc), s_rst, OP.mult)
                                vector.tensor_sub(s_t2, s_t2, s_mur)
                                vector.tensor_scalar(
                                    h_t[:, kc * TH:(kc + 1) * TH], s_t2,
                                    ca1_t[:, kc:kc + 1], sa_c(kc), OP.mult, OP.add)
                                vector.scalar_tensor_tensor(
                                    xh_t[:, kc * TH:(kc + 1) * TH],
                                    h_t[:, kc * TH:(kc + 1) * TH],
                                    ga_c(kc), xs(kc), OP.mult, OP.add)
                            sV.inc(vector.engine_nop())
                        v_h = sV.v

                        @blk.sync
                        def _(sync):
                            sync.wait_ge(sV.h, v_h)
                            if g >= 2:
                                pass
                            sync.dma_start(
                                out=AP(xh_spill, g * NCH * 128 * TH,
                                       [[TH, 128], [128 * TH, NCH], [1, TH]]),
                                in_=xh_t[:, :]).then_inc(sSp.h, 16)
                            sSp.v += 16

                        @blk.scalar
                        def _(scalar):
                            scalar.wait_ge(sV.h, v_h)
                            sSc.inc(scalar.activation(sq_t[:, :], h_t[:, :], ACT.Square))
                        sc_hsq = sSc.v

                        @blk.tensor
                        def _(tensor):
                            tensor.wait_ge(sSc.h, sc_hsq)
                            mm = None
                            for kc in range(NCH):
                                mm = tensor.matmul(ps[hb][:, :], ones_t[:, :],
                                                   h_t[:, kc * TH:(kc + 1) * TH],
                                                   start=(kc == 0), stop=(kc == NCH - 1))
                            for kc in range(NCH):
                                mm = tensor.matmul(ps[hb + 1][:, :], ones_t[:, :],
                                                   sq_t[:, kc * TH:(kc + 1) * TH],
                                                   start=(kc == 0), stop=(kc == NCH - 1))
                            sPE.inc(mm)
                        pe_hst = sPE.v

                        @blk.vector
                        def _(vector):
                            vector.wait_ge(sPE.h, pe_hst)
                            vector.tensor_scalar_mul(s_mu, ps[hb][:, :], 1.0 / D)
                            vector.tensor_scalar_mul(s_var, ps[hb + 1][:, :], 1.0 / D)
                            vector.tensor_mul(s_t2, s_mu, s_mu)
                            sV.inc(vector.tensor_sub(s_var, s_var, s_t2))
                        v_hvar = sV.v

                        @blk.scalar
                        def _(scalar):
                            scalar.wait_ge(sV.h, v_hvar)
                            sSc.inc(scalar.activation(s_rst, s_var, ACT.Sqrt,
                                                      bias=epsln_t[:, :]))
                        sc_hstd = sSc.v

                        @blk.vector
                        def _(vector):
                            vector.wait_ge(sSc.h, sc_hstd)
                            vector.wait_ge(sSp.h, sSp.v)
                            vector.reciprocal(s_rst, s_rst)
                            vector.tensor_mul(s_mur, s_mu, s_rst)
                            for kc in range(NCH):
                                hc = h_t[:, kc * TH:(kc + 1) * TH]
                                vector.tensor_tensor(s_t2, hc, s_rst, OP.mult)
                                vector.tensor_sub(s_t2, s_t2, s_mur)
                                vector.tensor_scalar(
                                    hn_t[:, kc * TT + st * TH:kc * TT + (st + 1) * TH],
                                    s_t2, gamma_t[:, kc:kc + 1], beta_t[:, kc:kc + 1],
                                    OP.mult, OP.add)
                            sV.inc(vector.engine_nop())

                        if tau == 0 and st == 0 and tap("h", [128, NCH * TH], BF16):
                            @blk.sync
                            def _(sync):
                                sync.wait_ge(sV.h, sV.v)
                                sync.dma_start(out=tap_outs["h"].ap(),
                                               in_=h_t[:, :]).then_inc(sO.h, 16)
                                sO.v += 16

                if tau == 0 and tap("hn", [128, NCH * TT], BF16):
                    with nc.Block() as blk:
                        @blk.sync
                        def _(sync):
                            sync.dma_start(out=tap_outs["hn"].ap(),
                                           in_=hn_t[:, :]).then_inc(sO.h, 16)
                            sO.v += 16

                RING = 4
                with nc.sbuf_tensor(f"w_t{tau}", [128, RING * NCH * 128], BF16) as w_t, \
                     nc.sbuf_tensor(f"quad_t{tau}", [128, 4 * TT], F32) as quad_t, \
                     nc.sbuf_tensor(f"rsc_t{tau}", [128, 2 * TT], F32) as rsc_t, \
                     nc.sbuf_tensor(f"tab_t{tau}", [128, 4 * TT], F32) as tab_t, \
                     nc.sbuf_tensor(f"sqs_t{tau}", [128, 2 * TT], BF16) as sqs_t, \
                     nc.sbuf_tensor(f"qev_t{tau}", [128, 2 * TT], BF16) as qev_t, \
                     nc.sbuf_tensor(f"wvec_t{tau}", [128, TT], F32) as wvec_t, \
                     nc.sbuf_tensor(f"junk_t{tau}", [128, TT], F32) as junk_t, \
                     nc.Block() as blk:

                    wslice = lambda r, kc: w_t[:, (r * NCH + kc) * 128:(r * NCH + kc + 1) * 128]
                    quad = lambda s: quad_t[:, s * TT:(s + 1) * TT]
                    quad_h = lambda s, th: quad_t[:, s * TT + th * TH:s * TT + (th + 1) * TH]
                    sqs = lambda r: sqs_t[:, r * TT:(r + 1) * TT]
                    sqs_h = lambda r, th: sqs_t[:, r * TT + th * TH:r * TT + (th + 1) * TH]
                    qev = lambda r: qev_t[:, r * TT:(r + 1) * TT]
                    qev_h = lambda r, th: qev_t[:, r * TT + th * TH:r * TT + (th + 1) * TH]
                    tabc = lambda q: tab_t[:, (q % 2) * 2 * TT:(q % 2) * 2 * TT + TT]
                    tabs = lambda q: tab_t[:, ((q % 2) * 2 + 1) * TT:((q % 2) * 2 + 2) * TT]

                    w_ready = {}
                    pe_blk = {}
                    ev_blk = {}
                    sqmm_done = {}
                    sq_done = {}
                    tab_ready = {}
                    rot_done = {}
                    qspill = {}
                    k2cnt = [0]
                    v2cnt = [0]
                    k2box = [0]

                    def qkv_bank(b, th):
                        if b >= 32:
                            return (2 * (b - 32) + th) % 8
                        return (2 * b + th) % 4

                    def emit_wdma(b):
                        @blk.gpsimd
                        def _(gpsimd):
                            if b >= RING:
                                gpsimd.wait_ge(sPE.h, pe_blk[b - RING])
                            gpsimd.dma_start(
                                out=w_t[:, (b % RING) * NCH * 128:(b % RING + 1) * NCH * 128],
                                in_=AP(w_in_d, b * 128 * NCH * 128,
                                       [[NCH * 128, 128], [1, NCH * 128]]),
                            ).then_inc(sW.h, 16)
                            sW.v += 16
                            w_ready[b] = sW.v

                    if "A2" in phases:
                        for b in range(RING):
                            emit_wdma(b)

                    for b in (range(NBLK) if "A2" in phases else []):
                        kind, idx = WIN_KINDS[b]

                        if kind == "KE":
                            q = idx

                            @blk.sync
                            def _(sync):
                                if q >= 2:
                                    sync.wait_ge(sV.h, rot_done[q - 2])
                                sync.dma_start(
                                    out=tabc(q),
                                    in_=AP(cos_d, q * 128 * TC + tau * TT,
                                           [[TC, 128], [1, TT]])).then_inc(sX.h, 16)
                                sX.v += 16
                                sync.dma_start(
                                    out=tabs(q),
                                    in_=AP(sin_d, q * 128 * TC + tau * TT,
                                           [[TC, 128], [1, TT]])).then_inc(sX.h, 16)
                                sX.v += 16
                                tab_ready[q] = sX.v

                        @blk.tensor
                        def _(tensor):
                            tensor.wait_ge(sW.h, w_ready[b])
                            if b >= 36:
                                tensor.wait_ge(sEv.h, ev_blk[b - 4])
                            elif b in (34, 35):
                                tensor.wait_ge(sV.h, k2box[0])
                            elif b >= 2:
                                tensor.wait_ge(sEv.h, ev_blk[b - 2])
                            mm = None
                            for kc in range(NCH):
                                lhs = wslice(b % RING, kc)
                                for th in range(2):
                                    mm = tensor.matmul(
                                        ps[qkv_bank(b, th)][:, :], lhs,
                                        hn_t[:, kc * TT + th * TH:kc * TT + (th + 1) * TH],
                                        start=(kc == 0), stop=(kc == NCH - 1))
                            pe_blk[b] = sPE.inc(mm)

                        if b + RING < NBLK:
                            emit_wdma(b + RING)

                        if kind == "Q" and "noq" not in phases:
                            i = idx

                            @blk.scalar
                            def _(scalar):
                                scalar.wait_ge(sPE.h, pe_blk[b])
                                if i >= 2:
                                    scalar.wait_ge(sSp.h, qspill[i - 2])
                                a = None
                                for th in range(2):
                                    a = scalar.activation(
                                        qev_h(i % 2, th), ps[qkv_bank(b, th)][:, :],
                                        ACT.Identity, bias=bqkv_t[:, b:b + 1], scale=1.0)
                                ev_blk[b] = sEv.inc(a)

                            @blk.sync
                            def _(sync):
                                sync.wait_ge(sEv.h, ev_blk[b])
                                for th in range(2):
                                    sync.dma_start(
                                        out=AP(q_spill,
                                               ((tau * 16 + i) * 2 + th) * 128 * TH,
                                               [[TH, 128], [1, TH]]),
                                        in_=qev_h(i % 2, th)).then_inc(sSp.h, 16)
                                    sSp.v += 16
                                qspill[i] = sSp.v
                        elif kind == "Q":
                            @blk.vector
                            def _(vector):
                                vector.wait_ge(sPE.h, pe_blk[b])
                                a = None
                                for th in range(2):
                                    a = vector.tensor_copy(
                                        junk_t[:, th * TH:(th + 1) * TH],
                                        ps[qkv_bank(b, th)][:, :])
                                ev_blk[b] = sEv.inc(a)
                        else:
                            slot = {"KE": 0, "KO": 1, "VE": 2, "VO": 3,
                                    "KP": 0, "VP": 1}[kind]

                            @blk.vector
                            def _(vector):
                                vector.wait_ge(sPE.h, pe_blk[b])
                                if kind in ("KP", "VP") and idx >= 1:
                                    vector.wait_ge(sSc.h, sq_done.get(("P", idx - 1), 0))
                                elif kind in ("KP", "VP"):
                                    vector.wait_ge(
                                        sSc.h,
                                        sq_done.get(12 if kind == "KP" else 13, 0))
                                elif idx >= 1:
                                    vector.wait_ge(sSc.h, sq_done.get(b - 4, 0))
                                a = None
                                for th in range(2):
                                    a = vector.tensor_scalar(
                                        quad_h(slot, th), ps[(2 * b + th) % 4][:, :],
                                        bqkv_t[:, b:b + 1], None, OP.add)
                                ev_blk[b] = sEv.inc(a)

                            if "nosq" not in phases:
                                @blk.scalar
                                def _(scalar):
                                    scalar.wait_ge(sEv.h, ev_blk[b])
                                    if b >= 2:
                                        scalar.wait_ge(sPE.h, sqmm_done.get(b - 2, 0))
                                    sSc.inc(scalar.activation(sqs(b % 2), quad(slot),
                                                              ACT.Square))
                                sq_done[b] = sSc.v
                                if kind in ("KP", "VP"):
                                    sq_done[("P", idx)] = sSc.v

                                @blk.tensor
                                def _(tensor):
                                    tensor.wait_ge(sSc.h, sq_done[b])
                                    cnt = k2cnt if kind.startswith("K") else v2cnt
                                    base = 4 if kind.startswith("K") else 6
                                    mm = None
                                    for th in range(2):
                                        mm = tensor.matmul(
                                            ps[base + th][:, :], ones_t[:, :],
                                            sqs_h(b % 2, th),
                                            start=(cnt[0] == 0), stop=(cnt[0] == 15))
                                    cnt[0] += 1
                                    sqmm_done[b] = sPE.inc(mm)
                            else:
                                sq_done[b] = sSc.v
                                if kind in ("KP", "VP"):
                                    sq_done[("P", idx)] = sSc.v

                        if kind == "VO" and "norot" not in phases:
                            q = idx

                            @blk.vector
                            def _(vector):
                                vector.wait_ge(sSc.h, sq_done[b])
                                vector.wait_ge(sX.h, tab_ready[q])
                                c = tabc(q); s = tabs(q)
                                r0 = rsc_t[:, 0:TT]; r1 = rsc_t[:, TT:2 * TT]
                                vector.tensor_tensor(r0, quad(0), s, OP.mult)
                                vector.tensor_tensor(r1, quad(0), c, OP.mult)
                                vector.tensor_tensor(quad(0), quad(1), s, OP.mult)
                                vector.tensor_sub(quad(0), r1, quad(0))
                                vector.tensor_tensor(quad(1), quad(1), c, OP.mult)
                                vector.tensor_add(quad(1), quad(1), r0)
                                vector.tensor_tensor(r0, quad(2), s, OP.mult)
                                vector.tensor_tensor(r1, quad(2), c, OP.mult)
                                vector.tensor_tensor(quad(2), quad(3), s, OP.mult)
                                vector.tensor_sub(quad(2), r1, quad(2))
                                vector.tensor_tensor(quad(3), quad(3), c, OP.mult)
                                vector.tensor_add(quad(3), quad(3), r0)
                                vector.tensor_tensor(
                                    p_t[:, q * TT:(q + 1) * TT], quad(0), quad(2), OP.mult)
                                a = vector.tensor_tensor(
                                    p_t[:, (4 + q) * TT:(5 + q) * TT], quad(1), quad(3),
                                    OP.mult)
                                rot_done[q] = sV.inc(a)

                        if kind == "VO" and "norot" in phases:
                            rot_done[idx] = sV.v
                        if kind == "VP" and "norot" not in phases:
                            j = idx

                            @blk.vector
                            def _(vector):
                                vector.wait_ge(sSc.h, sq_done[b])
                                sV.inc(vector.tensor_tensor(
                                    p_t[:, (8 + j) * TT:(9 + j) * TT],
                                    quad(0), quad(1), OP.mult))

                        if b == 31 and "noacc" not in phases and "nosq" not in phases:
                            @blk.vector
                            def _(vector):
                                vector.wait_ge(sPE.h, sqmm_done[31])
                                a = None
                                for th in range(2):
                                    a = vector.tensor_copy(
                                        rsc_t[:, th * TH:(th + 1) * TH],
                                        ps[4 + th][:, :])
                                    a = vector.tensor_copy(
                                        rsc_t[:, TT + th * TH:TT + (th + 1) * TH],
                                        ps[6 + th][:, :])
                                sV.inc(a)
                            v_k2 = sV.v
                            k2box[0] = v_k2

                            @blk.scalar
                            def _(scalar):
                                scalar.wait_ge(sV.h, v_k2)
                                scalar.activation(junk_t[:, :], rsc_t[:, 0:TT],
                                                  ACT.Sqrt)
                                sSc.inc(scalar.activation(
                                    wvec_t[:, :], rsc_t[:, TT:2 * TT], ACT.Sqrt))
                            sc_nrm = sSc.v

                            @blk.vector
                            def _(vector):
                                vector.wait_ge(sSc.h, sc_nrm)
                                vector.tensor_mul(wvec_t[:, :], wvec_t[:, :],
                                                  junk_t[:, :])
                                vector.reciprocal(wvec_t[:, :], wvec_t[:, :])
                                a = None
                                for kc in range(NCH):
                                    vector.tensor_tensor(
                                        junk_t[:, :], p_t[:, kc * TT:(kc + 1) * TT],
                                        wvec_t[:, :], OP.mult)
                                    a = vector.reduce_sum(
                                        acol_t[:, tau * NCH + kc:tau * NCH + kc + 1],
                                        junk_t[:, :], axis=AX.X)
                                sV.inc(a)

                    if tau == 0 and tap("kprime", [128, TT]):
                        @blk.sync
                        def _(sync):
                            sync.wait_ge(sV.h, sV.v)
                            sync.dma_start(out=tap_outs["kprime"].ap(),
                                           in_=wvec_t[:, :]).then_inc(sO.h, 16)
                            sO.v += 16

                    if tau == 0 and tap("pt", [128, NCH * TT], BF16):
                        @blk.sync
                        def _(sync):
                            sync.wait_ge(sV.h, sV.v)
                            sync.dma_start(out=tap_outs["pt"].ap(),
                                           in_=p_t[:, :]).then_inc(sO.h, 16)
                            sO.v += 16

                    if tau == 1 and tap("acol", [128, NCH * NTAU]):
                        @blk.sync
                        def _(sync):
                            sync.wait_ge(sV.h, sV.v)
                            sync.dma_start(out=tap_outs["acol"].ap(),
                                           in_=acol_t[:, :]).then_inc(sO.h, 16)
                            sO.v += 16

        with nc.Block() as blk:
            @blk.vector
            def _(vector):
                if "AF" not in phases:
                    return
                sV.inc(vector.tensor_add(afin_t[:, :], acol_t[:, 0:NCH],
                                         acol_t[:, NCH:2 * NCH]))
            a_part = sV.v

            @blk.gpsimd
            def _(gpsimd):
                if "AF" not in phases:
                    return
                gpsimd.wait_ge(sV.h, a_part)
                gpsimd.dma_start(out=cc_a_in.ap(), in_=afin_t[:, :]).then_inc(sW.h, 16)
                sW.v += 16
                gpsimd.wait_ge(sW.h, sW.v)
                sCC.inc(gpsimd.collective_compute(
                    "AllReduce", OP.add,
                    replica_groups=[[0, 1], [2, 3], [4, 5], [6, 7]],
                    ins=[cc_a_in.ap()], outs=[cc_a_out.ap()]))
                gpsimd.wait_ge(sCC.h, sCC.v)
                gpsimd.dma_start(out=afin_t[:, :], in_=cc_a_out.ap()).then_inc(sW.h, 16)
                sW.v += 16
            a_back = sW.v

            @blk.vector
            def _(vector):
                if "AF" not in phases:
                    return
                vector.wait_ge(sW.h, a_back)
                sV.inc(vector.tensor_scalar(afin_t[:, :], afin_t[:, :], 0.5, 0.5,
                                            OP.mult, OP.add))

            if tap("afin", [128, NCH]):
                @blk.sync
                def _(sync):
                    sync.wait_ge(sV.h, sV.v)
                    sync.dma_start(out=tap_outs["afin"].ap(),
                                   in_=afin_t[:, :]).then_inc(sO.h, 16)
                    sO.v += 16

        with nc.sbuf_tensor("x2_t", [128, NCH * TC], BF16) as x2_t:

            with nc.sbuf_tensor("aq_t", [128, NCH * TC], BF16) as aq_t, \
                 nc.sbuf_tensor("woutb", [128, 2 * NCH * 128], BF16) as woutb_t, \
                 nc.sbuf_tensor("xhb_t", [128, 2 * 4 * TH], BF16) as xhb_t, \
                 nc.sbuf_tensor("junkB", [128, TH], F32) as junkB_t, \
                 nc.Block() as blk:

                q_ready = []

                @blk.sync
                def _(sync):
                    if "B" not in phases:
                        q_ready.append(sX.v)
                        return
                    for tau in range(NTAU):
                        for th in range(2):
                            sync.dma_start(
                                out=AP(aq_t, tau * TT + th * TH,
                                       [[NCH * TC, 128], [TC, NCH], [1, TH]]),
                                in_=AP(q_spill, (tau * 32 + th) * 128 * TH,
                                       [[TH, 128], [2 * 128 * TH, 16], [1, TH]]),
                            ).then_inc(sX.h, 16)
                            sX.v += 16
                    q_ready.append(sX.v)

                @blk.vector
                def _(vector):
                    if "B" not in phases:
                        return
                    vector.wait_ge(sX.h, q_ready[0])
                    a = None
                    for kc in range(NCH):
                        a = vector.tensor_scalar_mul(
                            aq_t[:, kc * TC:(kc + 1) * TC],
                            aq_t[:, kc * TC:(kc + 1) * TC],
                            afin_t[:, kc:kc + 1])
                    sV.inc(a)
                aq_done = sV.v

                wout_ready = {}
                peB = {}
                evB = {}
                xh_ready = {}

                def emit_wout(ob):
                    @blk.gpsimd
                    def _(gpsimd):
                        if ob >= 2:
                            gpsimd.wait_ge(sPE.h, peB[ob - 2])
                        gpsimd.dma_start(
                            out=woutb_t[:, (ob % 2) * NCH * 128:(ob % 2 + 1) * NCH * 128],
                            in_=AP(w_out_d, ob * 128 * NCH * 128,
                                   [[NCH * 128, 128], [1, NCH * 128]]),
                        ).then_inc(sW.h, 16)
                        sW.v += 16
                        wout_ready[ob] = sW.v

                if "B" in phases:
                    emit_wout(0)
                    emit_wout(1)

                for ob in (range(16) if "B" in phases else []):
                    @blk.sync
                    def _(sync):
                        if ob >= 2:
                            sync.wait_ge(sEv.h, evB[ob - 2])
                        sync.dma_start(
                            out=xhb_t[:, (ob % 2) * 4 * TH:(ob % 2 + 1) * 4 * TH],
                            in_=AP(xh_spill, ob * 128 * TH,
                                   [[TH, 128], [NCH * 128 * TH, NG], [1, TH]]),
                        ).then_inc(sX.h, 16)
                        sX.v += 16
                        xh_ready[ob] = sX.v

                    @blk.tensor
                    def _(tensor):
                        tensor.wait_ge(sW.h, wout_ready[ob])
                        if ob == 0:
                            tensor.wait_ge(sV.h, aq_done)
                        if ob >= 2:
                            tensor.wait_ge(sEv.h, evB[ob - 2])
                        mm = None
                        base = 0 if ob % 2 == 0 else 4
                        for g in range(NG):
                            for kc in range(NCH):
                                mm = tensor.matmul(
                                    ps[base + g][:, :],
                                    woutb_t[:, ((ob % 2) * NCH + kc) * 128:((ob % 2) * NCH + kc + 1) * 128],
                                    aq_t[:, kc * TC + g * TH:kc * TC + (g + 1) * TH],
                                    start=(kc == 0), stop=(kc == NCH - 1))
                        peB[ob] = sPE.inc(mm)

                    if ob + 2 < 16:
                        emit_wout(ob + 2)

                    @blk.vector
                    def _(vector):
                        vector.wait_ge(sPE.h, peB[ob])
                        vector.wait_ge(sX.h, xh_ready[ob])
                        base = 0 if ob % 2 == 0 else 4
                        a = None
                        for g in range(NG):
                            vector.scalar_tensor_tensor(
                                junkB_t[:, :], ps[base + g][:, :],
                                modt[:, M_GA + ob:M_GA + ob + 1],
                                xhb_t[:, ((ob % 2) * 4 + g) * TH:((ob % 2) * 4 + g + 1) * TH],
                                OP.mult, OP.add)
                            a = vector.tensor_scalar_add(
                                x2_t[:, ob * TC + g * TH:ob * TC + (g + 1) * TH],
                                junkB_t[:, :], gab_t[:, ob:ob + 1])
                        evB[ob] = sEv.inc(a)

                if tap("x2", [128, NCH * TC], BF16):
                    @blk.sync
                    def _(sync):
                        sync.wait_ge(sEv.h, sEv.v)
                        sync.dma_start(out=tap_outs["x2"].ap(),
                                       in_=x2_t[:, :]).then_inc(sO.h, 16)
                        sO.v += 16

            with nc.sbuf_tensor("ht_t", [128, NCH * TC], BF16) as ht_t, \
                 nc.sbuf_tensor("wd128_t", [128, TC], BF16) as wd128_t, \
                 nc.sbuf_tensor("lgtm_t", [128, 2 * 128], F32) as lgtm_t, \
                 nc.sbuf_tensor("wdtm_t", [128, 128], F32) as wdtm_t, \
                 nc.sbuf_tensor("wdtm_bf", [128, 128], BF16) as wdtm_bf, \
                 nc.sbuf_tensor("wdrep_t", [128, 2 * 128], BF16) as wdrep_t, \
                 nc.sbuf_tensor("stR_t", [128, 6 * 16], F32) as stR_t, \
                 nc.sbuf_tensor("router_t", [128, NCH * 8], BF16) as router_t:

                with nc.sbuf_tensor("h2_t", [128, NCH * TH], F32) as h2_t, \
                     nc.sbuf_tensor("sqC_t", [128, NCH * TH], BF16) as sqC_t, \
                     nc.sbuf_tensor("stC_t", [128, 6 * TH], F32) as stC_t, \
                     nc.sbuf_tensor("lgfm_t", [8, TC], F32) as lgfm_t, \
                     nc.Block() as blk:

                    @blk.gpsimd
                    def _(gpsimd):
                        if "C1" not in phases:
                            return
                        gpsimd.dma_start(
                            out=router_t[:, :],
                            in_=AP(router_d, 0, [[NCH * 8, 128], [1, NCH * 8]]),
                        ).then_inc(sW.h, 16)
                        sW.v += 16
                    r_ready = sW.v

                    s_mu = stC_t[:, 0:TH]
                    s_var = stC_t[:, TH:2 * TH]
                    s_rst = stC_t[:, 2 * TH:3 * TH]
                    s_mur = stC_t[:, 3 * TH:4 * TH]
                    s_t2 = stC_t[:, 4 * TH:5 * TH]
                    s_rrm = stC_t[:, 5 * TH:6 * TH]

                    for g in (range(NG) if "C1" in phases else []):
                        x2s = lambda kc, g=g: x2_t[:, kc * TC + g * TH:kc * TC + (g + 1) * TH]

                        @blk.scalar
                        def _(scalar):
                            if g >= 1:
                                scalar.wait_ge(sV.h, sV.v)
                            sSc.inc(scalar.activation(
                                sqC_t[:, :],
                                AP(x2_t, g * TH, [[NCH * TC, 128], [TC, NCH], [1, TH]]),
                                ACT.Square))
                        sc_x2 = sSc.v

                        @blk.tensor
                        def _(tensor):
                            tensor.wait_ge(sSc.h, sc_x2)
                            mm = None
                            for kc in range(NCH):
                                mm = tensor.matmul(ps[0][:, :], ones_t[:, :], x2s(kc),
                                                   start=(kc == 0), stop=(kc == NCH - 1))
                            for kc in range(NCH):
                                mm = tensor.matmul(ps[1][:, :], ones_t[:, :],
                                                   sqC_t[:, kc * TH:(kc + 1) * TH],
                                                   start=(kc == 0), stop=(kc == NCH - 1))
                            sPE.inc(mm)
                        pe_x2 = sPE.v

                        @blk.vector
                        def _(vector):
                            vector.wait_ge(sPE.h, pe_x2)
                            vector.tensor_scalar_mul(s_mu, ps[0][:, :], 1.0 / D)
                            vector.tensor_scalar_mul(s_var, ps[1][:, :], 1.0 / D)
                            vector.tensor_mul(s_t2, s_mu, s_mu)
                            sV.inc(vector.tensor_sub(s_var, s_var, s_t2))
                        v_vx = sV.v

                        @blk.scalar
                        def _(scalar):
                            scalar.wait_ge(sV.h, v_vx)
                            sSc.inc(scalar.activation(s_rst, s_var, ACT.Sqrt,
                                                      bias=epsln_t[:, :]))
                        sc_sx = sSc.v

                        @blk.vector
                        def _(vector):
                            vector.wait_ge(sSc.h, sc_sx)
                            vector.reciprocal(s_rst, s_rst)
                            vector.tensor_mul(s_mur, s_mu, s_rst)
                            a = None
                            for kc in range(NCH):
                                vector.tensor_tensor(s_t2, x2s(kc), s_rst, OP.mult)
                                vector.tensor_sub(s_t2, s_t2, s_mur)
                                a = vector.tensor_scalar(
                                    h2_t[:, kc * TH:(kc + 1) * TH], s_t2,
                                    cb1_t[:, kc:kc + 1], sb_c(kc), OP.mult, OP.add)
                            sV.inc(a)
                        v_h2 = sV.v

                        @blk.scalar
                        def _(scalar):
                            scalar.wait_ge(sV.h, v_h2)
                            sSc.inc(scalar.activation(sqC_t[:, :], h2_t[:, :], ACT.Square))
                        sc_h2 = sSc.v

                        @blk.tensor
                        def _(tensor):
                            tensor.wait_ge(sSc.h, sc_h2)
                            mm = None
                            for kc in range(NCH):
                                mm = tensor.matmul(ps[2][:, :], ones_t[:, :],
                                                   sqC_t[:, kc * TH:(kc + 1) * TH],
                                                   start=(kc == 0), stop=(kc == NCH - 1))
                            sPE.inc(mm)
                        pe_rms = sPE.v

                        @blk.vector
                        def _(vector):
                            vector.wait_ge(sPE.h, pe_rms)
                            sV.inc(vector.tensor_scalar_mul(s_t2, ps[2][:, :],
                                                            1.0 / D))
                        v_rms = sV.v

                        @blk.scalar
                        def _(scalar):
                            scalar.wait_ge(sV.h, v_rms)
                            sSc.inc(scalar.activation(s_rrm, s_t2, ACT.Sqrt,
                                                      bias=epsrms_t[:, :]))
                        sc_rms = sSc.v

                        @blk.vector
                        def _(vector):
                            vector.wait_ge(sSc.h, sc_rms)
                            vector.reciprocal(s_rrm, s_rrm)
                            a = None
                            for kc in range(NCH):
                                a = vector.scalar_tensor_tensor(
                                    ht_t[:, kc * TC + g * TH:kc * TC + (g + 1) * TH],
                                    h2_t[:, kc * TH:(kc + 1) * TH],
                                    rmsw_t[:, kc:kc + 1], s_rrm, OP.mult, OP.mult)
                            sV.inc(a)
                    ht_done = sV.v

                    lg_pe = {}
                    lg_ev = {}
                    for g in (range(NG) if "C1" in phases
                              and "noC1b" not in phases else []):
                        @blk.tensor
                        def _(tensor):
                            if g == 0:
                                tensor.wait_ge(sV.h, ht_done)
                                tensor.wait_ge(sW.h, r_ready)
                            if g >= 2:
                                tensor.wait_ge(sEv.h, lg_ev[g - 2])
                            mm = None
                            for kc in range(NCH):
                                mm = tensor.matmul(
                                    ps[3 + (g % 2)][0:8, :],
                                    router_t[:, kc * 8:(kc + 1) * 8],
                                    ht_t[:, kc * TC + g * TH:kc * TC + (g + 1) * TH],
                                    start=(kc == 0), stop=(kc == NCH - 1))
                            lg_pe[g] = sPE.inc(mm)

                        @blk.vector
                        def _(vector):
                            vector.wait_ge(sPE.h, lg_pe[g])
                            lg_ev[g] = sEv.inc(vector.tensor_copy(
                                lgfm_t[0:8, g * TH:(g + 1) * TH],
                                ps[3 + (g % 2)][0:8, :]))
                    lg_done = sEv.v

                    tr_pe = {}
                    tr_ev = {}
                    for s16 in (range(16) if "C1" in phases
                                and "noC1c" not in phases else []):
                        @blk.tensor
                        def _(tensor):
                            if s16 == 0:
                                tensor.wait_ge(sEv.h, lg_done)
                            if s16 >= 2:
                                tensor.wait_ge(sEv.h, tr_ev[s16 - 2])
                            tensor.transpose(
                                ps[5 + (s16 % 2)][0:128, 0:8],
                                lgfm_t[0:8, s16 * 128:(s16 + 1) * 128],
                                ident_t[0:8, 0:8])
                            tr_pe[s16] = sPE.inc(nc.tensor.nop())

                        @blk.vector
                        def _(vector):
                            vector.wait_ge(sPE.h, tr_pe[s16])
                            tr_ev[s16] = sEv.inc(vector.tensor_copy(
                                lgtm_t[:, s16 * 8:(s16 + 1) * 8],
                                ps[5 + (s16 % 2)][0:128, 0:8]))
                    lgtm_done = sEv.v

                with nc.sbuf_tensor("m_t", [128, NFB * TC], BF16) as m_t, \
                     nc.sbuf_tensor("wgb_t", [128, 2 * NCH * 128], BF16) as wgb_t, \
                     nc.sbuf_tensor("wub_t", [128, 2 * NCH * 128], BF16) as wub_t, \
                     nc.sbuf_tensor("wdb_t", [128, 2 * NFB * 128], BF16) as wdb_t, \
                     nc.sbuf_tensor("sg_t", [128, TC], BF16) as sg_t, \
                     nc.sbuf_tensor("sgf_t", [128, 3 * TH], F32) as sgf_t, \
                     nc.sbuf_tensor("wvC_t", [128, TC], BF16) as wvC_t, \
                     nc.Block() as blk:

                    @blk.vector
                    def _(vector):
                        if "C1" not in phases or "noC1d" in phases:
                            return
                        vector.wait_ge(sEv.h, lgtm_done)
                        m1 = stR_t[:, 0:16]; m2 = stR_t[:, 16:32]
                        dl = stR_t[:, 32:48]; w1 = stR_t[:, 48:64]
                        w2 = stR_t[:, 64:80]
                        msk1 = lgtm_t[:, 128:256]
                        vector.engine_nop()
                        vector.engine_nop()
                        vector.reduce_max(m1, AP(lgtm_t, 0, [[256, 128], [8, 16], [1, 8]]),
                                          axis=AX.X)
                        vector.engine_nop()
                        vector.engine_nop()
                        vector.tensor_tensor(
                            msk1, lgtm_t[:, 0:128],
                            AP(stR_t, 0, [[96, 128], [1, 16], [0, 8]]), OP.is_equal)
                        vector.engine_nop()
                        vector.engine_nop()
                        vector.scalar_tensor_tensor(
                            wdtm_t[:, :], msk1, -1e30, lgtm_t[:, 0:128],
                            OP.mult, OP.add)
                        vector.engine_nop()
                        vector.engine_nop()
                        vector.reduce_max(m2, AP(wdtm_t, 0, [[128, 128], [8, 16], [1, 8]]),
                                          axis=AX.X)
                        vector.engine_nop()
                        vector.engine_nop()
                        vector.tensor_tensor(
                            wdtm_t[:, :], wdtm_t[:, :],
                            AP(stR_t, 16, [[96, 128], [1, 16], [0, 8]]), OP.is_equal)
                        sV.inc(vector.tensor_sub(dl, m1, m2))
                    v_top = sV.v

                    @blk.scalar
                    def _(scalar):
                        if "C1" not in phases or "noC1d" in phases:
                            return
                        scalar.wait_ge(sV.h, v_top)
                        sSc.inc(scalar.activation(stR_t[:, 48:64], stR_t[:, 32:48],
                                                  ACT.Tanh, scale=0.5))
                    sc_tanh = sSc.v

                    @blk.vector
                    def _(vector):
                        if "C1" not in phases or "noC1d" in phases:
                            return
                        vector.wait_ge(sSc.h, sc_tanh)
                        w1 = stR_t[:, 48:64]; w2 = stR_t[:, 64:80]
                        vector.tensor_scalar(w1, w1, 0.5, 0.5, OP.mult, OP.add)
                        vector.engine_nop()
                        vector.engine_nop()
                        vector.tensor_scalar(w2, w1, -1.0, 1.0, OP.mult, OP.add)
                        vector.engine_nop()
                        vector.engine_nop()
                        vector.tensor_tensor(
                            lgtm_t[:, 128:256], lgtm_t[:, 128:256],
                            AP(stR_t, 48, [[96, 128], [1, 16], [0, 8]]), OP.mult)
                        vector.tensor_tensor(
                            wdtm_t[:, :], wdtm_t[:, :],
                            AP(stR_t, 64, [[96, 128], [1, 16], [0, 8]]), OP.mult)
                        vector.engine_nop()
                        vector.engine_nop()
                        sV.inc(vector.tensor_add(wdtm_t[:, :], wdtm_t[:, :],
                                                 lgtm_t[:, 128:256]))
                    wd_done = sV.v

                    if tap("wdtm", [128, 128]):
                        @blk.sync
                        def _(sync):
                            sync.wait_ge(sV.h, wd_done)
                            sync.dma_start(out=tap_outs["wdtm"].ap(),
                                           in_=wdtm_t[:, :]).then_inc(sO.h, 16)
                            sO.v += 16

                    def emit_wd128():
                        @blk.scalar
                        def _(scalar):
                            if "C1" not in phases or "noC1e" in phases:
                                return
                            scalar.wait_ge(sV.h, wd_done)
                            sSc.inc(scalar.copy(wdtm_bf[:, :], wdtm_t[:, :]))
                        wdbf_done = sSc.v

                        for s16 in (range(16) if "C1" in phases
                                    and "noC1e" not in phases else []):
                            @blk.vector
                            def _(vector):
                                if s16 == 0:
                                    vector.wait_ge(sSc.h, wdbf_done)
                                if s16 >= 2:
                                    vector.wait_ge(sPE.h, bt_pe[s16 - 2])
                                bt_rep[s16] = sV.inc(vector.tensor_copy(
                                    wdrep_t[:, (s16 % 2) * 128:(s16 % 2 + 1) * 128],
                                    AP(wdtm_bf, s16 * 8,
                                       [[128, 128], [0, 16], [1, 8]])))

                            @blk.tensor
                            def _(tensor):
                                tensor.wait_ge(sV.h, bt_rep[s16])
                                if s16 < 2:
                                    tensor.wait_ge(sEv.h, ev_m[(0, NFB - 1)])
                                if s16 >= 2:
                                    tensor.wait_ge(sEv.h, bt_ev[s16 - 2])
                                mm = tensor.matmul(
                                    ps[5 + (s16 % 2)][0:128, 0:128],
                                    wdrep_t[:, (s16 % 2) * 128:(s16 % 2 + 1) * 128],
                                    identbf_t[:, :], start=True, stop=True)
                                bt_pe[s16] = sPE.inc(mm)

                            @blk.vector
                            def _(vector):
                                vector.wait_ge(sPE.h, bt_pe[s16])
                                bt_ev[s16] = sEv.inc(vector.tensor_copy(
                                    wd128_t[:, s16 * 128:(s16 + 1) * 128],
                                    ps[5 + (s16 % 2)][0:128, 0:128]))
                        wdfm_box[0] = sEv.v

                    bt_rep = {}
                    bt_pe = {}
                    bt_ev = {}
                    wdfm_box = [0]

                    experts = ["sh"] + list(range(E))
                    wg_rdy = {}
                    wu_rdy = {}
                    wd_rdy = {}
                    pe_g = {}
                    pe_u = {}
                    sc_sg = {}
                    ev_m = {}
                    pe_z = {}
                    ev_z = {}
                    wv_rdy = {}
                    last_ev_z = [0]
                    last_pe_z_all = {}

                    def gdma(eix, fb):
                        e = experts[eix]

                        @blk.gpsimd
                        def _(gpsimd):
                            slot = (eix * NFB + fb) % 2
                            if eix * NFB + fb >= 2:
                                prev = (eix * NFB + fb) - 2
                                pe_ix, pfb = divmod(prev, NFB)
                                gpsimd.wait_ge(sPE.h, pe_u[(pe_ix, pfb)])
                            for dram, buf, rdy in ((wg_d if e != "sh" else swg_d, wgb_t, wg_rdy),
                                                   (wu_d if e != "sh" else swu_d, wub_t, wu_rdy)):
                                off = ((e * NFB + fb) if e != "sh" else fb) * 128 * NCH * 128
                                gpsimd.dma_start(
                                    out=buf[:, slot * NCH * 128:(slot + 1) * NCH * 128],
                                    in_=AP(dram, off, [[NCH * 128, 128], [1, NCH * 128]]),
                                ).then_inc(sW.h, 16)
                                sW.v += 16
                                rdy[(eix, fb)] = sW.v

                    def ddma(eix, db):
                        e = experts[eix]

                        @blk.gpsimd
                        def _(gpsimd):
                            slot = (eix * 16 + db) % 2
                            if eix * 16 + db >= 2:
                                prev = (eix * 16 + db) - 2
                                pe_ix, pdb = divmod(prev, 16)
                                gpsimd.wait_ge(sPE.h, last_pe_z_all[(pe_ix, pdb)])
                            off = ((e * 16 + db) if e != "sh" else db) * 128 * NFB * 128
                            gpsimd.dma_start(
                                out=wdb_t[:, slot * NFB * 128:(slot + 1) * NFB * 128],
                                in_=AP(wd_d if e != "sh" else swd_d, off,
                                       [[NFB * 128, 128], [1, NFB * 128]]),
                            ).then_inc(sW.h, 16)
                            sW.v += 16
                            wd_rdy[(eix, db)] = sW.v

                    if "C2" in phases:
                        gdma(0, 0)
                        gdma(0, 1)

                    last_ev_z_all = {}

                    for eix, e in (list(enumerate(experts)) if "C2" in phases else []):
                        if eix == 1:
                            emit_wd128()
                        if e != "sh" and "noC2wv" in phases:
                            wv_rdy[eix] = sEv.v
                        elif e != "sh":
                            wv_pe = {}
                            wv_ev = {}
                            for g in range(NG):
                                @blk.tensor
                                def _(tensor):
                                    if g == 0:
                                        tensor.wait_ge(sEv.h, ev_m[(eix - 1, NFB - 1)])
                                        if eix == 1:
                                            tensor.wait_ge(sEv.h, wdfm_box[0])
                                    else:
                                        tensor.wait_ge(sEv.h, wv_ev[g - 1])
                                    mm = tensor.matmul(
                                        ps[6][:, :],
                                        sel16_t[:, e * 128:(e + 1) * 128],
                                        wd128_t[:, g * TH:(g + 1) * TH],
                                        start=True, stop=True)
                                    wv_pe[g] = sPE.inc(mm)

                                @blk.vector
                                def _(vector):
                                    vector.wait_ge(sPE.h, wv_pe[g])
                                    wv_ev[g] = sEv.inc(vector.tensor_copy(
                                        wvC_t[:, g * TH:(g + 1) * TH], ps[6][:, :]))
                            wv_rdy[eix] = sEv.v

                        for fb in range(NFB):
                            slot = (eix * NFB + fb) % 2

                            @blk.tensor
                            def _(tensor):
                                tensor.wait_ge(sW.h, wg_rdy[(eix, fb)])
                                if eix > 0 and fb == 0:
                                    tensor.wait_ge(sPE.h, last_pe_z_all[(eix - 1, 15)])
                                    tensor.wait_ge(sEv.h, last_ev_z_all[eix - 1])
                                if (eix, fb) != (0, 0):
                                    prev = (eix * NFB + fb) - 1
                                    pix, pfb = divmod(prev, NFB)
                                    tensor.wait_ge(sSc.h, sc_sg[(pix, pfb)])
                                mm = None
                                for kc in range(NCH):
                                    lhs = wgb_t[:, (slot * NCH + kc) * 128:(slot * NCH + kc + 1) * 128]
                                    for g in range(NG):
                                        mm = tensor.matmul(
                                            ps[g][:, :], lhs,
                                            ht_t[:, kc * TC + g * TH:kc * TC + (g + 1) * TH],
                                            start=(kc == 0), stop=(kc == NCH - 1))
                                pe_g[(eix, fb)] = sPE.inc(mm)

                            @blk.scalar
                            def _(scalar):
                                scalar.wait_ge(sPE.h, pe_g[(eix, fb)])
                                if (eix, fb) != (0, 0):
                                    prev = (eix * NFB + fb) - 1
                                    pix, pfb = divmod(prev, NFB)
                                    scalar.wait_ge(sEv.h, ev_m[(pix, pfb)])
                                sl = lambda i: sgf_t[:, (i % 3) * TH:(i % 3 + 1) * TH]
                                sgo = lambda g: sg_t[:, g * TH:(g + 1) * TH]
                                ident = lambda g: scalar.activation(
                                    sl(g), ps[g][:, :], ACT.Identity,
                                    bias=zcol_t[:, :], scale=1.0)
                                ident(0)
                                ident(1)
                                ident(2)
                                scalar.activation(sgo(0), sl(0), ACT.Silu)
                                ident(3)
                                scalar.activation(sgo(1), sl(1), ACT.Silu)
                                scalar.activation(sgo(2), sl(2), ACT.Silu)
                                a = scalar.activation(sgo(3), sl(3), ACT.Silu)
                                sc_sg[(eix, fb)] = sSc.inc(a)

                            @blk.tensor
                            def _(tensor):
                                tensor.wait_ge(sW.h, wu_rdy[(eix, fb)])
                                if (eix, fb) != (0, 0):
                                    prev = (eix * NFB + fb) - 1
                                    pix, pfb = divmod(prev, NFB)
                                    tensor.wait_ge(sEv.h, ev_m[(pix, pfb)])
                                if e != "sh" and fb == 0:
                                    tensor.wait_ge(sEv.h, wv_rdy[eix])
                                mm = None
                                for kc in range(NCH):
                                    lhs = wub_t[:, (slot * NCH + kc) * 128:(slot * NCH + kc + 1) * 128]
                                    for g in range(NG):
                                        mm = tensor.matmul(
                                            ps[4 + g][:, :], lhs,
                                            ht_t[:, kc * TC + g * TH:kc * TC + (g + 1) * TH],
                                            start=(kc == 0), stop=(kc == NCH - 1))
                                pe_u[(eix, fb)] = sPE.inc(mm)

                            if eix * NFB + fb + 2 < len(experts) * NFB:
                                nxt = eix * NFB + fb + 2
                                gdma(nxt // NFB, nxt % NFB)

                            @blk.vector
                            def _(vector):
                                vector.wait_ge(sPE.h, pe_u[(eix, fb)])
                                vector.wait_ge(sSc.h, sc_sg[(eix, fb)])
                                if e != "sh":
                                    vector.wait_ge(sEv.h, wv_rdy[eix])
                                a = None
                                for g in range(NG):
                                    dst = m_t[:, fb * TC + g * TH:fb * TC + (g + 1) * TH]
                                    if e == "sh" or "noC2wv" in phases:
                                        a = vector.tensor_tensor(
                                            dst, ps[4 + g][:, :],
                                            sg_t[:, g * TH:(g + 1) * TH], OP.mult)
                                    else:
                                        vector.tensor_tensor(
                                            dst, ps[4 + g][:, :],
                                            sg_t[:, g * TH:(g + 1) * TH], OP.mult)
                                        a = vector.tensor_tensor(
                                            dst, dst,
                                            wvC_t[:, g * TH:(g + 1) * TH], OP.mult)
                                ev_m[(eix, fb)] = sEv.inc(a)

                        if "noC2dn" in phases:
                            for db in range(16):
                                last_pe_z_all[(eix, db)] = sPE.v
                            last_ev_z_all[eix] = sEv.v
                            continue
                        ddma(eix, 0)
                        ddma(eix, 1)
                        for db in range(16):
                            slot = (eix * 16 + db) % 2

                            @blk.tensor
                            def _(tensor):
                                tensor.wait_ge(sW.h, wd_rdy[(eix, db)])
                                if db == 0:
                                    tensor.wait_ge(sEv.h, ev_m[(eix, NFB - 1)])
                                    tensor.wait_ge(sSc.h, sc_sg[(eix, NFB - 1)])
                                for g in range(NG):
                                    if db >= 1:
                                        tensor.wait_ge(sEv.h, ev_z[(eix, db - 1, g)])
                                    mm = None
                                    for kc in range(NFB):
                                        mm = tensor.matmul(
                                            ps[g][:, :],
                                            wdb_t[:, (slot * NFB + kc) * 128:(slot * NFB + kc + 1) * 128],
                                            m_t[:, kc * TC + g * TH:kc * TC + (g + 1) * TH],
                                            start=(kc == 0), stop=(kc == NFB - 1))
                                    pe_z[(eix, db, g)] = sPE.inc(mm)

                            @blk.vector
                            def _(vector):
                                for g in range(NG):
                                    vector.wait_ge(sPE.h, pe_z[(eix, db, g)])
                                    ev_z[(eix, db, g)] = sEv.inc(
                                        vector.scalar_tensor_tensor(
                                            x2_t[:, db * TC + g * TH:db * TC + (g + 1) * TH],
                                            ps[g][:, :], gb_c(db),
                                            x2_t[:, db * TC + g * TH:db * TC + (g + 1) * TH],
                                            OP.mult, OP.add))
                            last_pe_z_all[(eix, db)] = pe_z[(eix, db, NG - 1)]

                            if eix == len(experts) - 1:
                                @blk.gpsimd
                                def _(gpsimd):
                                    gpsimd.wait_ge(sEv.h, ev_z[(eix, db, NG - 1)])
                                    gpsimd.dma_start(
                                        out=AP(out_fm, db * 128 * TC,
                                               [[TC, 128], [1, TC]]),
                                        in_=x2_t[:, db * TC:(db + 1) * TC],
                                    ).then_inc(sO.h, 16)
                                    sO.v += 16

                            if db + 2 < 16:
                                ddma(eix, db + 2)
                        last_ev_z_all[eix] = ev_z[(eix, 15, NG - 1)]

                    @blk.gpsimd
                    def _(gpsimd):
                        if "C2" not in phases:
                            return
                        gpsimd.wait_ge(sO.h, sO.v)

        with nc.Block() as blk:
            @blk.gpsimd
            def _(gpsimd):
                for sem in (sW, sX, sSp, sCC, sO):
                    if sem.v:
                        gpsimd.wait_ge(sem.h, sem.v)

    return nc, tap_outs


def kernel(**inputs):
    in_maps = make_host_inputs(inputs)
    nc, _ = build_program()
    res = run_bass_kernel_spmd(nc, in_maps, core_ids=list(range(8)))
    return assemble_output(res.results)
